# version 02
# baseline (speedup 1.0000x reference)
import contextlib
import numpy as np
import ml_dtypes
import concourse.bass as bass
import concourse.mybir as mybir
from concourse.bass_utils import run_bass_kernel_spmd

F32 = mybir.dt.float32
BF16 = mybir.dt.bfloat16
AF = mybir.ActivationFunctionType
OP = mybir.AluOpType
AX = mybir.AxisListType.X

D = 1024
H = 16
DH = 64
NLAYER = 6
DFF = 4096
NE = 8
CH = 128
NFC = D // 128
LN_EPS = 1e-5
ATT_EPS = 1e-6
GATE_EPS = 1e-4

ENGINES = ("sync", "tensor", "vector", "scalar", "gpsimd")


class Sem:
    def __init__(self, nc, stack, name):
        self.h = stack.enter_context(nc.semaphore(name))
        self.n = 0
        self.name = name
    def bump(self, k=1):
        self.n += k
        return self.n


def merge_waits(waits):
    d = {}
    for s, t in waits:
        if t is None:
            continue
        d[s] = max(d.get(s, 0), t)
    return [(s, t) for s, t in d.items() if t > 0]


class Res:
    def __init__(self, view=None, name=""):
        self.v = view
        self.name = name
        self.ready = []
        self.frees = []
    def produced(self, *deps):
        self.ready = list(deps)
        self.frees = []
    def consumed(self, sem, t):
        self.frees.append((sem, t))
    def rw(self):
        return merge_waits(self.ready + self.frees)
    def r(self):
        return merge_waits(self.ready)


class P:
    def __init__(self, nc, stack):
        self.nc = nc
        self.q = {e: [] for e in ENGINES}
        self.dA = Sem(nc, stack, "dA")
        self.dW = Sem(nc, stack, "dW")
        self.dWp = [Sem(nc, stack, "dW0"), Sem(nc, stack, "dW1")]
        self.dS = Sem(nc, stack, "dS")
        self.pe = Sem(nc, stack, "pe")
        self.gp = Sem(nc, stack, "gp")
        self.dv = Sem(nc, stack, "dv")
        self.ac = Sem(nc, stack, "ac")
        self.bar = Sem(nc, stack, "bar")

    def emit(self, eng, waits, fn, inc=None, inck=1):
        w = merge_waits(waits)
        t = inc.bump(inck) if inc is not None else None
        def op(e, w=w, fn=fn, inc=inc, inck=inck):
            for s, c in w:
                e.wait_ge(s.h, c)
            last = fn(e)
            if inc is not None:
                assert last is not None, "fn must return last instruction"
                last.then_inc(inc.h, inck)
        self.q[eng].append(op)
        return (inc, t) if inc is not None else None

    def seq(self, eng, waits, steps, inc):
        mark = None
        for i, fn in enumerate(steps):
            mark = self.emit(eng, (waits if i == 0 else [mark]), fn, inc=inc)
        return mark

    def barrier(self):
        self.bar.bump(5)
        tgt = self.bar.n
        for eng in ENGINES:
            def op(e, tgt=tgt):
                e.nop().then_inc(self.bar.h, 1)
                e.wait_ge(self.bar.h, tgt)
            self.q[eng].append(op)

    def realize(self):
        with self.nc.Block() as block:
            for eng in ENGINES:
                ops = self.q[eng]
                def body(e, ops=ops):
                    for f in ops:
                        f(e)
                getattr(block, eng)(body)


class Arena:
    def __init__(self, nc, stack, name, kib):
        self.bytes = kib * 1024
        self.t = stack.enter_context(nc.sbuf_tensor(name, [128, self.bytes // 2], BF16))
        self.h32 = self.t.bitcast(F32)
        self.h16 = self.t
    def v(self, off, parts, elems, dtype, p0=0):
        esz = 4 if dtype == F32 else 2
        assert off % esz == 0 and off + elems * esz <= self.bytes, (off, elems, esz)
        h = self.h32 if dtype == F32 else self.h16
        return h[p0:p0 + parts, off // esz: off // esz + elems]


def build(nseq, debug_taps=(), upto="final"):
    NCHUNK = nseq // CH
    GA = min(512, nseq)
    NGA = nseq // GA
    CPG = GA // CH
    GB = min(256, nseq)
    NGB = nseq // GB

    stack = contextlib.ExitStack()
    nc = bass.Bass("TRN2", target_bir_lowering=False)
    p = P(nc, stack)
    dA, dW, dS, pe, dv, ac = p.dA, p.dW, p.dS, p.pe, p.dv, p.ac

    din = {}
    def dtin(name, shape, dtype=F32):
        din[name] = nc.dram_tensor(name, list(shape), dtype, kind="ExternalInput").ap()
        return din[name]
    x_in = dtin("x", (nseq, D))
    WqA = dtin("Wq", (NLAYER, D, D)); WkA = dtin("Wk", (NLAYER, D, D))
    WvA = dtin("Wv", (NLAYER, D, D)); WoA = dtin("Wo", (NLAYER, D, D))
    bq8A = dtin("bq8", (NLAYER, 128, 8)); bk8A = dtin("bk8", (NLAYER, 128, 8))
    bo8A = dtin("bo8", (NLAYER, 128, 8))
    bvrowA = dtin("bvrow", (NLAYER, 1, D))
    W1A = dtin("W1", (NLAYER, D, DFF)); W2A = dtin("W2", (NLAYER, DFF, D))
    b1tA = dtin("b1t", (NLAYER, 128, 32)); b2tA = dtin("b2t", (NLAYER, 128, 8))
    g1rA = dtin("g1r", (NLAYER, 1, D)); be1rA = dtin("be1r", (NLAYER, 1, D))
    g2rA = dtin("g2r", (NLAYER, 1, D)); be2rA = dtin("be2r", (NLAYER, 1, D))
    gfr = dtin("gfr", (1, D)); bfr = dtin("bfr", (1, D))
    mw1A = dtin("mw1", (NE, D, DFF)); mw2A = dtin("mw2", (NE, DFF, D))
    mb1tA = dtin("mb1t", (NE, 128, 32)); mb2tA = dtin("mb2t", (NE, 128, 8))
    wgredT_in = dtin("wgredT", (128, 1024))
    wg = dtin("wg", (8, 16))
    c_idf = dtin("c_idf", (128, 128))
    c_tril16 = dtin("c_tril16", (128, 512), BF16)
    c_id16 = dtin("c_id16", (128, 128), BF16)
    c_epad = dtin("c_epad", (128, 128), BF16)
    c_e2pad = dtin("c_e2pad", (128, 128), BF16)
    c_onescol = dtin("c_onescol", (128, 128), BF16)
    c_onesrow = dtin("c_onesrow", (128, 128), BF16)

    out_t = nc.dram_tensor("out", [nseq, D], F32, kind="ExternalOutput").ap()

    def dscratch(name, shape, dtype):
        kind = "ExternalOutput" if name in debug_taps else "Internal"
        return nc.dram_tensor(name, list(shape), dtype, kind=kind).ap()
    XN32 = dscratch("XN32", (D, nseq), F32)
    XN16 = dscratch("XN16", (D, nseq), BF16)
    XS = [dscratch("XA", (D, nseq), F32), dscratch("XB", (D, nseq), F32)]
    Y32 = dscratch("Y32", (D, nseq), F32)
    S16 = dscratch("S16", (D, nseq), BF16)
    ACC = dscratch("ACC", (D, nseq), BF16)
    GT = dscratch("GT", (8, nseq), F32)

    AR = Arena(nc, stack, "AR", 206)
    v_idf = AR.v(0, 128, 128, F32)
    v_tril = AR.v(512, 128, 512, BF16)
    v_id16 = AR.v(1536, 128, 128, BF16)
    v_epad = AR.v(1792, 128, 128, BF16)
    v_e2pad = AR.v(2048, 128, 128, BF16)
    v_onescol = AR.v(2304, 128, 128, BF16)
    v_onesrow = AR.v(2560, 128, 128, BF16)
    v_wgu = AR.v(2816, 128, 128, F32)
    v_eps = AR.v(3840, 128, 4, F32)
    stg = [AR.v(4096, 128, 1024, F32), AR.v(8192, 128, 1024, F32)]
    stgr = [Res(stg[0], "stg0"), Res(stg[1], "stg1")]
    SM = 12288
    v_bq = AR.v(SM + 0, 128, 8, F32);  v_bqn = AR.v(SM + 32, 128, 8, F32)
    v_bk = AR.v(SM + 64, 128, 8, F32); v_bkn = AR.v(SM + 96, 128, 8, F32)
    v_bo = AR.v(SM + 128, 128, 8, F32)
    v_b2 = AR.v(SM + 160, 128, 8, F32)
    v_b1 = AR.v(SM + 192, 128, 32, F32)
    v_z = AR.v(SM + 320, 128, 8, F32)
    v_gax = AR.v(SM + 512, 128, 1024, BF16)
    v_gbx = AR.v(SM + 2560, 128, 1024, BF16)
    v_abA = AR.v(SM + 4608, 128, 512, BF16)
    v_abB = AR.v(SM + 5632, 128, 512, BF16)
    v_bvrep = AR.v(SM + 6656, 128, 1024, BF16)
    v_wgredT = AR.v(SM + 8704, 128, 1024, F32)
    v_wguT = AR.v(SM + 12800, 128, 128, F32)
    v_grow = AR.v(SM + 13312, 1, 1024, F32)
    v_berow = AR.v(SM + 17408, 1, 1024, F32)
    v_rowM = AR.v(SM + 21504, 1, 512, F32)
    v_rowV = AR.v(SM + 23552, 1, 512, F32)
    v_rowR = AR.v(SM + 25600, 1, 512, F32)
    SM_END = SM + 27648
    STAGE = 40960
    STAGE_SZ = 206 * 1024 - STAGE
    assert SM_END <= STAGE
    def sv(off, parts, elems, dtype, p0=0):
        esz = 4 if dtype == F32 else 2
        assert 0 <= off and off + elems * esz <= STAGE_SZ, (off, elems)
        return AR.v(STAGE + off, parts, elems, dtype, p0=p0)

    banks = [stack.enter_context(nc.psum_tensor(f"pb{i}", [128, 512], F32)) for i in range(8)]
    banks16 = [b.bitcast(BF16) for b in banks]
    bres = [Res(banks[i], f"bank{i}") for i in range(8)]

    r_gax = Res(v_gax, "gax"); r_gbx = Res(v_gbx, "gbx")
    r_abA = Res(v_abA, "abA"); r_abB = Res(v_abB, "abB")
    r_bvrep = Res(v_bvrep, "bvrep")
    r_bq = Res(v_bq, "bq"); r_bqn = Res(v_bqn, "bqn")
    r_bk = Res(v_bk, "bk"); r_bkn = Res(v_bkn, "bkn")
    r_bo = Res(v_bo, "bo"); r_b1 = Res(v_b1, "b1"); r_b2 = Res(v_b2, "b2")
    r_z = Res(v_z, "z")
    r_grow = Res(v_grow, "grow"); r_berow = Res(v_berow, "berow")
    r_rows = Res(None, "rows")

    def dma_load(dst_res, fn_dma, extra=()):
        w = merge_waits(list(dst_res.frees) + list(extra) + [(dA, dA.n)])
        t = dA.bump(16)
        def op(e, w=w, fn=fn_dma, t=t):
            for s, c in w:
                e.wait_ge(s.h, c)
            fn(e).then_inc(dA.h, 16)
        p.q["sync"].append(op)
        dst_res.produced((dA, t))
        return (dA, t)

    def fm_slice(dram, t0, T):
        return dram.rearrange("(c p) n -> p c n", p=128)[:, :, t0:t0 + T]

    def fmv(view):
        return view.rearrange("p (c t) -> p c t", c=NFC)

    def store(dram_ap_fn, src_view, waits, consumers=()):
        w = merge_waits(list(waits) + [(dS, dS.n)])
        t = dS.bump(16)
        def op(g, w=w, t=t):
            for s, c in w:
                g.wait_ge(s.h, c)
            g.dma_start(out=dram_ap_fn(), in_=src_view).then_inc(dS.h, 16)
        p.q["gpsimd"].append(op)
        for r in consumers:
            r.consumed(dS, t)
        return (dS, t)

    def store_fm(dram, t0, T, src_res):
        return store(lambda dram=dram, t0=t0, T=T: fm_slice(dram, t0, T),
                     src_res.v.rearrange("p (c t) -> p c t", c=NFC),
                     src_res.r(), consumers=(src_res,))

    wload = {"phase": 0}
    def load_weight(dram3, lidx, rows, cols, dest16, dest_res, jmajor=False):
        marks = {}
        order = ([(r, j) for j in range(cols // 1024) for r in range(rows // 128)]
                 if jmajor else
                 [(r, j) for r in range(rows // 128) for j in range(cols // 1024)])
        for (r, j) in order:
            if True:
                half = wload["phase"] % 2
                wload["phase"] += 1
                sres = stgr[half]
                dWh = p.dWp[half]
                w = merge_waits(sres.frees + dest_res.frees)
                t = dWh.bump(16)
                def opd(s, r=r, j=j, half=half, w=w, t=t, dram3=dram3, lidx=lidx,
                        cols=cols, dWh=dWh):
                    for sm, c in w:
                        s.wait_ge(sm.h, c)
                    src = dram3[lidx, r * 128:(r + 1) * 128, j * 1024:(j + 1) * 1024]
                    s.dma_start(out=stg[half], in_=src).then_inc(dWh.h, 16)
                p.q["scalar"].append(opd)
                sres.produced((dWh, t))
                cast = p.emit("scalar", sres.r(),
                              lambda e, half=half, r=r, j=j, cols=cols, dest16=dest16:
                              e.copy(out=dest16[:, r * cols + j * 1024:r * cols + (j + 1) * 1024],
                                     in_=stg[half]),
                              inc=ac)
                sres.consumed(ac, cast[1])
                marks[(r, j)] = cast
        dest_res.produced((ac, ac.n))
        return marks

    def load_small(dram2d, dst_view, dst_res=None):
        w = merge_waits((dst_res.frees if dst_res else []) + [(dA, dA.n)])
        t = dA.bump(16)
        def op(e, w=w, t=t):
            for s, c in w:
                e.wait_ge(s.h, c)
            e.dma_start(out=dst_view, in_=dram2d).then_inc(dA.h, 16)
        p.q["sync"].append(op)
        if dst_res is not None:
            dst_res.produced((dA, t))
        return (dA, t)

    def fm3(view, e):
        pass

    def f_consts(sync):
        for vv, src in [(v_idf, c_idf), (v_tril, c_tril16), (v_id16, c_id16),
                        (v_epad, c_epad), (v_e2pad, c_e2pad), (v_onescol, c_onescol),
                        (v_onesrow, c_onesrow), (v_wgredT, wgredT_in)]:
            sync.dma_start(out=vv, in_=src).then_inc(dA.h, 16)
    p.q["sync"].append(f_consts)
    dA.bump(8 * 16)
    consts_t = dA.n
    for eng in ("tensor", "vector", "scalar", "gpsimd"):
        p.q[eng].append(lambda e, t=consts_t: e.wait_ge(dA.h, t))
    zinit = p.seq("vector", [], [
        lambda v: (v.memset(v_eps[:, 0:1], LN_EPS),
                   v.memset(v_eps[:, 1:2], ATT_EPS),
                   v.memset(v_gax, 0.0),
                   v.memset(v_gbx, 0.0),
                   v.memset(v_abA, 0.0),
                   v.memset(v_abB, 0.0),
                   v.memset(v_wguT, 0.0))[-1],
        lambda v: (v.memset(v_abB[32:33, :], 1.0),
                   v.memset(v_abB[64:65, :], 1.0))[-1],
    ], dv)
    p.barrier()

    def stage_ln0():
        bufs = [Res(sv(i * 4096, 128, 1024, F32), f"ln0x{i}") for i in range(2)]
        r_x32 = [Res(sv(8192 + i * 4096, 128, 1024, F32), f"xn32_{i}") for i in range(2)]
        r_x16 = [Res(sv(16384 + i * 2048, 128, 1024, BF16), f"xn16_{i}") for i in range(2)]
        stats = [Res(sv(20480 + i * 128, 128, 16, F32), f"st{i}") for i in range(2)]
        for i in range(NCHUNK):
            bx = bufs[i % 2]
            x32, x16 = r_x32[i % 2], r_x16[i % 2]
            st = stats[i % 2]
            sview = st.v
            dma_load(bx, lambda e, i=i, bx=bx:
                     e.dma_start(out=bx.v, in_=x_in[i * 128:(i + 1) * 128, :]))
            s1 = p.emit("vector", bx.r() + st.rw(), lambda v, bx=bx, sview=sview: (
                v.bn_stats(sview[:, 0:6], bx.v[:, 0:512]),
                v.bn_stats(sview[:, 6:12], bx.v[:, 512:1024]))[-1], inc=dv)
            s2 = p.emit("vector", [s1], lambda v, sview=sview:
                        v.bn_aggr(sview[:, 12:14], sview[:, 0:12]), inc=dv)
            s3 = p.emit("scalar", [s2], lambda s, sview=sview:
                        s.activation(out=sview[:, 14:15], in_=sview[:, 13:14],
                                     func=AF.Sqrt, bias=v_eps[:, 0:1], scale=1.0), inc=ac)
            s4 = p.emit("vector", [s3], lambda v, sview=sview:
                        v.reciprocal(out=sview[:, 14:15], in_=sview[:, 14:15]), inc=dv)
            s5 = p.emit("vector", [s4], lambda v, bx=bx, sview=sview:
                        v.tensor_scalar(out=bx.v, in0=bx.v, scalar1=sview[:, 12:13],
                                        scalar2=sview[:, 14:15], op0=OP.subtract,
                                        op1=OP.mult), inc=dv)
            st.produced(s5)
            lastev = None
            for half in range(2):
                bk = bres[half]
                trm = p.emit("tensor", [s5] + bk.rw(),
                             lambda t, bx=bx, half=half, bk=bk: [
                    t.transpose(bk.v[0:128, c * 128:(c + 1) * 128],
                                bx.v[:, (half * 4 + c) * 128:(half * 4 + c + 1) * 128],
                                v_idf) for c in range(4)][-1], inc=pe)
                bk.produced(trm)
                ev = p.emit("vector", bk.r() + (x32.rw() + x16.rw() if half == 0 else []),
                            lambda v, half=half, bk=bk, x32=x32, x16=x16: (
                    v.tensor_copy(out=x32.v[:, half * 512:(half + 1) * 512], in_=bk.v[:, 0:512]),
                    v.tensor_copy(out=x16.v[:, half * 512:(half + 1) * 512],
                                  in_=bk.v[:, 0:512]))[-1], inc=dv)
                bk.consumed(dv, ev[1])
                bx.consumed(pe, trm[1])
                lastev = ev
            x32.produced(lastev); x16.produced(lastev)
            store_fm(XN32, i * 128, 128, x32)
            store_fm(XN16, i * 128, 128, x16)
        p.q["gpsimd"].append(lambda e, t=dS.n: e.wait_ge(dS.h, t))

    def stage_gate():
        wgr = Res(None, "wg")
        load_small(wg, v_wgu[0:8, 0:16], wgr)
        pm = p.seq("vector", wgr.r(), [
            lambda v: v.tensor_tensor(out=v_wgu[0:8, 16:32], in0=v_wgu[0:8, 0:16],
                                      in1=v_wgu[0:8, 0:16], op=OP.mult),
            lambda v: v.reduce_sum(out=v_wgu[0:8, 32:33], in_=v_wgu[0:8, 16:32], axis=AX),
        ], dv)
        pm2 = p.emit("scalar", [pm], lambda s:
                     s.activation(out=v_wgu[0:8, 33:34], in_=v_wgu[0:8, 32:33],
                                  func=AF.Sqrt, bias=0.0, scale=1.0), inc=ac)
        pm3 = p.seq("vector", [pm2], [
            lambda v: v.tensor_scalar_max(out=v_wgu[0:8, 33:34], in0=v_wgu[0:8, 33:34],
                                          scalar1=GATE_EPS),
            lambda v: v.reciprocal(out=v_wgu[0:8, 34:35], in_=v_wgu[0:8, 33:34]),
            lambda v: v.tensor_scalar_mul(out=v_wgu[0:8, 48:64], in0=v_wgu[0:8, 0:16],
                                          scalar1=v_wgu[0:8, 34:35]),
        ], dv)
        tr = p.emit("tensor", [pm3] + bres[7].rw(), lambda t:
                    t.transpose(banks[7][0:16, 0:8], v_wgu[0:8, 48:64], v_idf[0:8, 0:8]),
                    inc=pe)
        bres[7].produced(tr)
        pm4 = p.emit("vector", bres[7].r(), lambda v:
                     v.tensor_copy(out=v_wguT[0:16, 0:8], in_=banks[7][0:16, 0:8]), inc=dv)
        bres[7].consumed(dv, pm4[1])
        wgu_ok = pm4

        T = GA
        xbuf = [Res(sv(i * 16384, 128, NFC * T, F32), f"gx{i}") for i in range(2)]
        lgrow = [Res(sv(32768 + i * 2048, 8, T, F32), f"lgr{i}") for i in range(2)]
        xrr = [Res(sv(36864 + i * 2048, 128, T, F32), f"xr{i}") for i in range(2)]
        gtr = [Res(sv(40960 + i * 2048, 8, T, F32), f"gtb{i}") for i in range(2)]
        v_mx = sv(45056, 128, 8, F32)
        v_mxn = sv(45088, 128, 8, F32)
        v_sum = sv(45120, 128, 8, F32)
        v_sc = sv(45152, 128, 8, F32)
        r_eq = Res(sv(45184, 128, 8, F32), "eq")
        r_ex = Res(sv(45216, 128, 8, F32), "ex")
        zi = p.emit("vector", [x.rw() for x in xrr][0] + xrr[1].rw(), lambda v: (
            v.memset(xrr[0].v, 0.0), v.memset(xrr[1].v, 0.0))[-1], inc=dv)
        for xr in xrr:
            xr.consumed(dv, zi[1])
        for g in range(nseq // T):
            bx = xbuf[g % 2]
            dma_load(bx, lambda e, g=g, bx=bx, T=T:
                     e.dma_start(out=fmv(bx.v), in_=fm_slice(XN32, g * T, T)))
            bk = bres[0]
            mm1 = p.emit("tensor", bx.r() + bk.rw(), lambda t, bx=bx, T=T: [
                t.matmul(banks[0][0:128, 0:T], v_wgredT[:, c * 128:(c + 1) * 128],
                         bx.v[:, c * T:(c + 1) * T], start=(c == 0), stop=(c == NFC - 1))
                for c in range(NFC)][-1], inc=pe)
            bk.produced(mm1)
            bx.consumed(pe, mm1[1])
            xr = xrr[g % 2]
            ev1 = p.emit("vector", bk.r() + xr.rw(), lambda v, xr=xr, T=T:
                         v.tensor_copy(out=xr.v[0:16, 0:T], in_=banks[0][0:16, 0:T]), inc=dv)
            bk.consumed(dv, ev1[1])
            xr.produced(ev1)
            bk1 = bres[1]
            mm2 = p.emit("tensor", xr.r() + bk1.rw() + [wgu_ok], lambda t, xr=xr, T=T:
                         t.matmul(banks[1][0:128, 0:T], v_wguT[0:128, 0:128],
                                  xr.v[0:128, 0:T], start=True, stop=True), inc=pe)
            bk1.produced(mm2)
            xr.consumed(pe, mm2[1])
            lg = lgrow[g % 2]
            ev2 = p.emit("vector", bk1.r() + lg.rw(), lambda v, lg=lg, T=T:
                         v.tensor_copy(out=lg.v, in_=banks[1][0:8, 0:T]), inc=dv)
            bk1.consumed(dv, ev2[1])
            lg.produced(ev2)
            gt = gtr[g % 2]
            last_evb = None
            for tc in range(T // 128):
                bk2 = bres[2 + tc % 2]
                trm = p.emit("tensor", lg.r() + bk2.rw(), lambda t, lg=lg, tc=tc, bk2=bk2:
                             t.transpose(bk2.v[0:128, 0:8],
                                         lg.v[:, tc * 128:(tc + 1) * 128],
                                         v_idf[0:8, 0:8]), inc=pe)
                bk2.produced(trm)
                if tc == T // 128 - 1:
                    lg.consumed(pe, trm[1])
                sm1 = p.seq("vector", bk2.r(), [
                    lambda v, bk2=bk2, tc=tc:
                        v.reduce_max(out=v_mx[:, tc:tc + 1], in_=bk2.v[0:128, 0:8], axis=AX),
                    lambda v, tc=tc:
                        v.tensor_scalar_mul(out=v_mxn[:, tc:tc + 1], in0=v_mx[:, tc:tc + 1],
                                            scalar1=-1.0)], dv)
                sm2 = p.emit("vector", [sm1] + r_eq.rw(), lambda v, bk2=bk2, tc=tc:
                             v.tensor_scalar(out=r_eq.v, in0=bk2.v[0:128, 0:8],
                                             scalar1=v_mx[:, tc:tc + 1], scalar2=None,
                                             op0=OP.is_equal), inc=dv)
                sm3 = p.emit("scalar", [sm1] + r_ex.rw(), lambda s, bk2=bk2, tc=tc:
                             s.activation(out=r_ex.v, in_=bk2.v[0:128, 0:8], func=AF.Exp,
                                          bias=v_mxn[:, tc:tc + 1], scale=1.0), inc=ac)
                bk2.consumed(ac, sm3[1]); bk2.consumed(dv, sm2[1])
                r_ex.produced(sm3)
                sm6 = p.seq("vector", r_ex.r() + [sm2], [
                    lambda v, tc=tc: v.reduce_sum(out=v_sum[:, tc:tc + 1], in_=r_ex.v, axis=AX),
                    lambda v, tc=tc: v.reciprocal(out=v_sc[:, tc:tc + 1], in_=v_sum[:, tc:tc + 1]),
                    lambda v, tc=tc: v.tensor_scalar_mul(out=r_eq.v, in0=r_eq.v,
                                                         scalar1=v_sc[:, tc:tc + 1])], dv)
                r_ex.consumed(dv, sm6[1])
                r_eq.produced(sm6)
                bk3 = bres[4 + tc % 2]
                trb = p.emit("tensor", r_eq.r() + bk3.rw(), lambda t, bk3=bk3:
                             t.transpose(bk3.v[0:8, 0:128], r_eq.v, v_idf), inc=pe)
                bk3.produced(trb)
                r_eq.consumed(pe, trb[1])
                evb = p.emit("vector", bk3.r() + (gt.rw() if tc == 0 else []),
                             lambda v, bk3=bk3, gt=gt, tc=tc:
                             v.tensor_copy(out=gt.v[:, tc * 128:(tc + 1) * 128],
                                           in_=bk3.v[0:8, 0:128]), inc=dv)
                bk3.consumed(dv, evb[1])
                last_evb = evb
            gt.produced(last_evb)
            store(lambda g=g, T=T: GT[:, g * T:(g + 1) * T], gt.v, gt.r(), consumers=(gt,))
        p.q["gpsimd"].append(lambda e, t=dS.n: e.wait_ge(dS.h, t))

    def prep_ln_rows(growd, berowd, lidx):
        load_small(growd[lidx], v_grow, r_grow)
        load_small(berowd[lidx], v_berow, r_berow)
        m1 = p.seq("vector", r_grow.r() + r_gax.rw() + r_gbx.rw() + r_abA.rw(), [
            lambda v: v.tensor_copy(out=v_gax[0:1, :], in_=v_grow),
            lambda v: (v.tensor_copy(out=v_gax[32:33, :], in_=v_gax[0:1, :]),
                       v.tensor_copy(out=v_gbx[0:1, :], in_=v_gax[0:1, :]),
                       v.tensor_sub(out=v_abA[0:1, 0:512], in0=v_grow[0:1, 0:512],
                                    in1=v_gax[0:1, 0:512]),
                       v.tensor_sub(out=v_abA[0:1, 0:512] if False else v_abB[0:1, 0:512],
                                    in0=v_grow[0:1, 512:1024],
                                    in1=v_gax[0:1, 512:1024]))[-1],
            lambda v: (v.tensor_copy(out=v_gax[64:65, 0:512], in_=v_abA[0:1, 0:512]),
                       v.tensor_copy(out=v_gax[64:65, 512:1024], in_=v_abB[0:1, 0:512]))[-1],
        ], dv)
        m2 = p.seq("vector", r_berow.r() + r_abB.rw() + [m1], [
            lambda v: v.tensor_copy(out=v_gbx[32:33, :], in_=v_berow),
            lambda v: (v.tensor_sub(out=v_abA[0:1, 0:512], in0=v_berow[0:1, 0:512],
                                    in1=v_gbx[0:1, 0:512] if False else v_gbx[0:1, 0:512]),)[-1]
                      if False else
                      (v.tensor_copy(out=v_abA[0:1, 0:512], in_=v_berow[0:1, 0:512]),
                       v.tensor_copy(out=v_abB[0:1, 0:512], in_=v_berow[0:1, 512:1024]))[-1],
            lambda v: (v.tensor_sub(out=v_abA[0:1, 0:512], in0=v_berow[0:1, 0:512],
                                    in1=v_abA[0:1, 0:512]),
                       v.tensor_sub(out=v_abB[0:1, 0:512], in0=v_berow[0:1, 512:1024],
                                    in1=v_abB[0:1, 0:512]))[-1],
            lambda v: (v.tensor_copy(out=v_gbx[64:65, 0:512], in_=v_abA[0:1, 0:512]),
                       v.tensor_copy(out=v_gbx[64:65, 512:1024], in_=v_abB[0:1, 0:512]))[-1],
        ], dv)
        r_abA.consumed(dv, m2[1]); r_abB.consumed(dv, m2[1])
        r_gax.produced(m1); r_gbx.produced(m2)
        r_grow.consumed(dv, m1[1]); r_berow.consumed(dv, m2[1])
        return merge_waits([m1, m2])

    def ln_feature_major(r32res, T, rb16, sq16, bk_sums, bk_sumsq, bk_repA, bk_repB,
                         t1scr, extra_rows=None):
        c1 = p.emit("gpsimd", r32res.r() + rb16.rw(), lambda s:
                    s.tensor_copy(out=rb16.v, in_=r32res.v), inc=p.gp)
        rb16.produced(c1)
        c2 = p.emit("scalar", r32res.r() + sq16.rw(), lambda s: (
            s.square(out=sq16.v, in_=r32res.v)), inc=ac)
        sq16.produced(c2)
        ms = p.emit("tensor", rb16.r() + bk_sums.rw(), lambda t: [
            t.matmul(bk_sums.v[0:128, 0:T], v_onescol, rb16.v[:, c * T:(c + 1) * T],
                     start=(c == 0), stop=(c == NFC - 1)) for c in range(NFC)][-1], inc=pe)
        bk_sums.produced(ms)
        rb16.consumed(pe, ms[1])
        mq = p.emit("tensor", sq16.r() + bk_sumsq.rw(), lambda t: [
            t.matmul(bk_sumsq.v[0:128, 0:T], v_onescol, sq16.v[:, c * T:(c + 1) * T],
                     start=(c == 0), stop=(c == NFC - 1)) for c in range(NFC)][-1], inc=pe)
        bk_sumsq.produced(mq)
        sq16.consumed(pe, mq[1])
        d1 = p.emit("vector", bk_sums.r() + r_rows.rw(), lambda v:
                    v.tensor_scalar_mul(out=v_rowM[:, 0:T], in0=bk_sums.v[0:1, 0:T],
                                        scalar1=1.0 / D), inc=dv)
        bk_sums.consumed(dv, d1[1])
        d2 = p.emit("vector", [d1], lambda v:
                    v.tensor_mul(out=v_rowV[:, 0:T], in0=v_rowM[:, 0:T],
                                 in1=v_rowM[:, 0:T]), inc=dv)
        d3 = p.emit("vector", [d2] + bk_sumsq.r(), lambda v:
                    v.scalar_tensor_tensor(out=v_rowV[:, 0:T], in0=bk_sumsq.v[0:1, 0:T],
                                           scalar=1.0 / D, in1=v_rowV[:, 0:T],
                                           op0=OP.mult, op1=OP.subtract), inc=dv)
        bk_sumsq.consumed(dv, d3[1])
        d4 = p.emit("scalar", [d3], lambda s:
                    s.activation(out=v_rowR[:, 0:T], in_=v_rowV[:, 0:T], func=AF.Sqrt,
                                 bias=v_eps[0:1, 0:1], scale=1.0), inc=ac)
        d5 = p.emit("vector", [d4], lambda v:
                    v.reciprocal(out=v_rowR[:, 0:T], in_=v_rowR[:, 0:T]), inc=dv)
        d6 = p.emit("vector", [d5] + r_abA.rw(), lambda v:
                    v.tensor_copy(out=v_abA[0:1, 0:T], in_=v_rowR[:, 0:T]), inc=dv)
        d7 = p.emit("vector", [d6], lambda v: (
            v.tensor_sub(out=v_abA[32:33, 0:T], in0=v_rowR[:, 0:T], in1=v_abA[0:1, 0:T]),
            v.tensor_copy(out=v_abA[64:65, 0:T], in_=v_abA[0:1, 0:T]))[-1], inc=dv)
        d8 = p.emit("vector", [d5] + r_abB.rw(), lambda v:
                    v.scalar_tensor_tensor(out=v_abB[0:1, 0:T], in0=v_rowM[:, 0:T],
                                           scalar=-1.0, in1=v_rowR[:, 0:T],
                                           op0=OP.mult, op1=OP.mult), inc=dv)
        r_abA.produced(d7); r_abB.produced(d8)
        r_rows.produced(d7, d8)
        lastap = None
        for c in range(NFC):
            bA = bk_repA[c % 2]
            bB = bk_repB[c % 2]
            mA = p.emit("tensor", r_abA.r() + r_gax.r() + bA.rw(), lambda t, c=c, bA=bA:
                        t.matmul(bA.v[0:128, 0:T], v_gax[:, c * 128:(c + 1) * 128],
                                 v_abA[:, 0:T], start=True, stop=True), inc=pe)
            bA.produced(mA)
            mB = p.emit("tensor", r_abB.r() + r_gbx.r() + bB.rw(), lambda t, c=c, bB=bB:
                        t.matmul(bB.v[0:128, 0:T], v_gbx[:, c * 128:(c + 1) * 128],
                                 v_abB[:, 0:T], start=True, stop=True), inc=pe)
            bB.produced(mB)
            a1 = p.emit("vector", bA.r() + t1scr.rw() + r32res.r(),
                        lambda v, c=c, bA=bA: v.tensor_mul(
                            out=t1scr.v, in0=r32res.v[:, c * T:(c + 1) * T],
                            in1=bA.v[0:128, 0:T]), inc=dv)
            bA.consumed(dv, a1[1])
            t1scr.produced(a1)
            a2 = p.emit("vector", [a1] + bB.r(), lambda v, c=c, bB=bB:
                        v.tensor_add(out=r32res.v[:, c * T:(c + 1) * T], in0=t1scr.v,
                                     in1=bB.v[0:128, 0:T]), inc=dv)
            bB.consumed(dv, a2[1])
            t1scr.consumed(dv, a2[1])
            lastap = a2
        r32res.produced(lastap)
        return lastap

    def fm_emit(src_psum_res, dst16, dst_cols, bias_col, biasneg_col, scr16, T):
        e1 = p.emit("scalar", src_psum_res.r() + scr16.rw(), lambda s:
                    s.activation(out=scr16.v[:, 0:T], in_=src_psum_res.v[0:128, 0:T],
                                 func=AF.Relu, bias=biasneg_col, scale=-1.0), inc=ac)
        scr16.produced(e1)
        e2 = p.emit("scalar", [e1], lambda s:
                    s.activation(out=scr16.v[:, T:2 * T], in_=scr16.v[:, 0:T],
                                 func=AF.Exp, bias=0.0, scale=-1.0), inc=ac)
        e3 = p.emit("scalar", [e2] + src_psum_res.r(), lambda s:
                    s.activation(out=dst16.v[:, dst_cols[0]:dst_cols[1]],
                                 in_=src_psum_res.v[0:128, 0:T],
                                 func=AF.Relu, bias=bias_col, scale=1.0), inc=ac)
        src_psum_res.consumed(ac, e3[1])
        e4 = p.emit("vector", [e3], lambda v:
                    v.tensor_add(out=dst16.v[:, dst_cols[0]:dst_cols[1]],
                                 in0=dst16.v[:, dst_cols[0]:dst_cols[1]],
                                 in1=scr16.v[:, T:2 * T]), inc=dv)
        scr16.consumed(dv, e4[1])
        return e4

    def stage_attn(l):
        SRC = XN32 if l == 0 else XS[(l - 1) % 2]
        wq16 = sv(0, 128, 8 * 1024, BF16); wk16 = sv(16384, 128, 8 * 1024, BF16)
        wv16 = sv(32768, 128, 8 * 1024, BF16); wo16 = sv(49152, 128, 8 * 1024, BF16)
        r_wq = Res(wq16, "wq"); r_wk = Res(wk16, "wk")
        r_wv = Res(wv16, "wv"); r_wo = Res(wo16, "wo")
        load_weight(WqA, l, D, D, wq16, r_wq)
        load_weight(WkA, l, D, D, wk16, r_wk)
        load_weight(WvA, l, D, D, wv16, r_wv)
        load_weight(WoA, l, D, D, wo16, r_wo)
        load_small(bq8A[l], v_bq, r_bq)
        load_small(bk8A[l], v_bk, r_bk)
        load_small(bo8A[l], v_bo, r_bo)
        nb = p.emit("vector", r_bq.r() + r_bk.r() + r_bqn.rw() + r_bkn.rw(), lambda v: (
            v.tensor_scalar_mul(out=v_bqn, in0=v_bq, scalar1=-1.0),
            v.tensor_scalar_mul(out=v_bkn, in0=v_bk, scalar1=-1.0))[-1], inc=dv)
        r_bqn.produced(nb); r_bkn.produced(nb)
        prep_ln_rows(g1rA, be1rA, l)
        bvld = load_small(bvrowA[l], v_grow, r_grow)
        last_bv = None
        for halfc in range(2):
            cp = p.emit("vector", [bvld] + r_abA.rw() + ([last_bv] if last_bv else []),
                        lambda v, halfc=halfc:
                        v.tensor_copy(out=v_abA[0:1, 0:512],
                                      in_=v_grow[0:1, halfc * 512:(halfc + 1) * 512]),
                        inc=dv)
            bkx = bres[6 + halfc]
            mm = p.emit("tensor", [cp] + bkx.rw(), lambda t, bkx=bkx:
                        t.matmul(bkx.v[0:128, 0:512], v_onesrow[0:128, :],
                                 v_abA[0:128, 0:512], start=True, stop=True), inc=pe)
            bkx.produced(mm)
            r_abA.consumed(pe, mm[1])
            ev = p.emit("vector", bkx.r() + (r_bvrep.rw() if halfc == 0 else []),
                        lambda v, halfc=halfc, bkx=bkx:
                        v.tensor_copy(out=v_bvrep[:, halfc * 512:(halfc + 1) * 512],
                                      in_=bkx.v[0:128, 0:512]), inc=dv)
            bkx.consumed(dv, ev[1])
            last_bv = ev
        r_bvrep.produced(last_bv)
        r_grow.consumed(dv, last_bv[1])

        xg = [Res(sv(65536 + i * 16384, 128, NFC * GA, F32), f"xg{i}") for i in range(2)]
        r32 = Res(sv(98304, 128, NFC * GA, F32), "r32")
        xg16 = Res(sv(114688, 128, NFC * GA, BF16), "xg16")
        fmq = Res(sv(122880, 128, NFC * GA, BF16), "fmq")
        fmk = Res(sv(131072, 128, NFC * GA, BF16), "fmk")
        osb = Res(sv(139264, 128, NFC * GA, BF16), "osb")
        ve = Res(sv(147456, 128, 1024, BF16), "ve")
        vo = Res(sv(149504, 128, 1024, BF16), "vo")
        fmk_e = Res(sv(151552, 128, 1024, BF16), "fmk_e")
        fmk_o = Res(sv(153600, 128, 1024, BF16), "fmk_o")
        fmk_tm = Res(sv(155648, 128, 1024, BF16), "fmk_tm")
        p16 = Res(sv(157696, 128, 1024, BF16), "p16")
        rd16 = Res(sv(159744, 128, 1024, BF16), "rd16")
        kvp = Res(sv(161792, 128, 1024, BF16), "kvp")
        w16 = Res(sv(163840, 128, 1024, BF16), "w16")
        rdrep = Res(sv(165888, 128, 1024, BF16), "rdrep")
        fmscr = p16
        t1scr = Res(sv(163840, 128, GA, F32), "t1scr")
        amask_e = Res(sv(167936, 128, 512, BF16), "amask_e")
        amask_o = Res(sv(168960, 128, 512, BF16), "amask_o")
        zl = p.emit("vector", merge_waits(fmk_e.rw() + fmk_o.rw() + kvp.rw()
                                          + rd16.rw() + r_z.rw() + ve.rw() + vo.rw()),
                    lambda v: (
            v.memset(fmk_e.v, 0.0),
            v.memset(fmk_o.v, 0.0),
            v.memset(kvp.v, 0.0),
            v.memset(rd16.v, 0.0),
            v.memset(ve.v, 0.0),
            v.memset(vo.v, 0.0),
            v.memset(v_z, 0.0))[-1], inc=dv)
        for r in (fmk_e, fmk_o, kvp, rd16, ve, vo):
            r.produced(zl)
        zmark = zl
        r_z.produced(zl)

        bKV = [bres[0], bres[1]]

        for g in range(NGA - 1, -1, -1):
            t0 = g * GA
            bx = xg[g % 2]
            dma_load(bx, lambda e, t0=t0, bx=bx:
                     e.dma_start(out=fmv(bx.v), in_=fm_slice(SRC, t0, GA)))
            cst = p.emit("scalar", bx.r() + xg16.rw(), lambda s, bx=bx:
                         s.copy(out=xg16.v, in_=bx.v), inc=ac)
            xg16.produced(cst)
            fm_last = None
            for (wsb, wres, dst, bias, biasn, br, brn) in (
                    (wq16, r_wq, fmq, v_bq, v_bqn, r_bq, r_bqn),
                    (wk16, r_wk, fmk, v_bk, v_bkn, r_bk, r_bkn)):
                for dc in range(NFC):
                    bkp = bres[2 + dc % 2]
                    mm = p.emit("tensor", xg16.r() + wres.r() + bkp.rw() + br.r() + brn.r(),
                                lambda t, wsb=wsb, dc=dc, bkp=bkp: [
                        t.matmul(bkp.v[0:128, 0:GA],
                                 wsb[:, kc * 1024 + dc * 128: kc * 1024 + (dc + 1) * 128],
                                 xg16.v[:, kc * GA:(kc + 1) * GA],
                                 start=(kc == 0), stop=(kc == NFC - 1))
                        for kc in range(NFC)][-1], inc=pe)
                    bkp.produced(mm)
                    fm_last = fm_emit(bkp, dst, (dc * GA, (dc + 1) * GA),
                                      bias[:, dc:dc + 1], biasn[:, dc:dc + 1], fmscr, GA)
            fmq.produced(fm_last); fmk.produced(fm_last)

            for ci in range(CPG - 1, -1, -1):
                n_global = g * CPG + ci
                first = (n_global == NCHUNK - 1)
                ccols = (ci * 128, (ci + 1) * 128)
                vlast = None
                for half in range(2):
                    bkp = bres[4 + half]
                    mm = p.emit("tensor", xg16.r() + r_wv.r() + bkp.rw(),
                                lambda t, ci=ci, half=half, bkp=bkp: [
                        t.matmul(bkp.v[0:128, 0:512],
                                 xg16.v[:, kc * GA + ci * 128: kc * GA + (ci + 1) * 128],
                                 wv16[:, kc * 1024 + half * 512: kc * 1024 + (half + 1) * 512],
                                 start=(kc == 0), stop=(kc == NFC - 1))
                        for kc in range(NFC)][-1], inc=pe)
                    bkp.produced(mm)
                    for par, dstv in ((0, ve), (1, vo)):
                        evm = p.emit("vector", bkp.r() + r_bvrep.r() + dstv.rw(),
                                     lambda v, half=half, par=par, dstv=dstv, bkp=bkp:
                            v.scalar_tensor_tensor(
                                out=dstv.v[:, half * 512:(half + 1) * 512]
                                    .rearrange("p (b x) -> p b x", x=64)[:, par::2, :],
                                in0=bkp.v[0:128, 0:512]
                                    .rearrange("p (b x) -> p b x", x=64)[:, par::2, :],
                                scalar=1.0,
                                in1=v_bvrep[:, half * 512:(half + 1) * 512]
                                    .rearrange("p (b x) -> p b x", x=64)[:, par::2, :],
                                op0=OP.mult, op1=OP.add), inc=dv)
                        vlast = evm
                    bkp.consumed(dv, vlast[1])
                ve.produced(vlast); vo.produced(vlast)
                fsp = p.emit("vector", fmk.r() + fmk_e.rw() + fmk_o.rw(), lambda v, ci=ci: (
                    v.tensor_copy(
                        out=fmk_e.v[0:64, :].rearrange("p (b x) -> p b x", x=128),
                        in_=fmk.v[0:64, :].rearrange("p (b x) -> p b x", x=GA)[:, :, ci * 128:(ci + 1) * 128]),
                    v.tensor_copy(
                        out=fmk_o.v[64:128, :].rearrange("p (b x) -> p b x", x=128),
                        in_=fmk.v[64:128, :].rearrange("p (b x) -> p b x", x=GA)[:, :, ci * 128:(ci + 1) * 128]))[-1],
                    inc=dv)
                fmk_e.produced(fsp); fmk_o.produced(fsp)
                b4, b5 = bres[4], bres[5]
                trm = p.emit("tensor", fmk.r() + b4.rw(), lambda t, ci=ci: [
                    t.transpose(banks16[4][0:128, dc * 128:(dc + 1) * 128],
                                fmk.v[:, dc * GA + ci * 128: dc * GA + (ci + 1) * 128],
                                v_id16) for dc in range(NFC)][-1], inc=pe)
                b4.produced(trm)
                tev = p.emit("vector", b4.r() + fmk_tm.rw(), lambda v:
                             v.tensor_copy(out=fmk_tm.v, in_=banks16[4][0:128, 0:1024]),
                             inc=dv)
                b4.consumed(dv, tev[1])
                fmk_tm.produced(tev)
                if not first:
                    kevs = []
                    for kb in range(2):
                        for base in (0, 64):
                            km = p.emit("vector", bKV[kb].r() + kvp.rw(),
                                        lambda v, kb=kb, base=base:
                                v.tensor_copy(
                                    out=kvp.v[base:base + 64, kb * 512:(kb + 1) * 512]
                                        .rearrange("p (b x) -> p b x", x=128)[:, :, base:base + 64],
                                    in_=bKV[kb].v[base:base + 64, 0:512]
                                        .rearrange("p (b x) -> p b x", x=128)[:, :, base:base + 64]),
                                inc=dv)
                            kevs.append(km)
                            bKV[kb].consumed(dv, km[1])
                    kvp.produced(*kevs[-2:])
                num_banks = (bres[6], bres[7])
                for rnd in range(2):
                    bA, bB = bres[2], bres[3]
                    am = p.emit("tensor", fmk_e.r() + fmk_o.r() + fmq.r() + bA.rw() + bB.rw(),
                                lambda t, ci=ci, rnd=rnd, bA=bA, bB=bB: [
                        t.matmul(bA.v[0:128, i * 128:(i + 1) * 128] if par == 0 else
                                 bB.v[0:128, i * 128:(i + 1) * 128],
                                 (fmk_e if par == 0 else fmk_o).v[:, (rnd * 4 + i) * 128:(rnd * 4 + i + 1) * 128],
                                 fmq.v[:, (rnd * 4 + i) * GA + ci * 128:(rnd * 4 + i) * GA + (ci + 1) * 128],
                                 start=True, stop=True)
                        for i in range(4) for par in range(2)][-1], inc=pe)
                    bA.produced(am); bB.produced(am)
                    me = p.emit("vector", bA.r() + amask_e.rw(), lambda v:
                                v.tensor_mul(out=amask_e.v, in0=bA.v[0:128, 0:512],
                                             in1=v_tril), inc=dv)
                    bA.consumed(dv, me[1])
                    amask_e.produced(me)
                    mo = p.emit("vector", bB.r() + amask_o.rw(), lambda v:
                                v.tensor_mul(out=amask_o.v, in0=bB.v[0:128, 0:512],
                                             in1=v_tril), inc=dv)
                    bB.consumed(dv, mo[1])
                    amask_o.produced(mo)
                    nbk = num_banks[rnd]
                    numw = (amask_e.r() + amask_o.r() + ve.r() + vo.r() + nbk.rw()
                            + (kvp.r() + fmq.r() if not first else []))
                    def fnum(t, ci=ci, rnd=rnd, nbk=nbk, first=first):
                        last = None
                        for i in range(4):
                            dc = rnd * 4 + i
                            reg = nbk.v[0:128, i * 128:(i + 1) * 128]
                            last = t.matmul(reg, ve.v[:, dc * 128:(dc + 1) * 128],
                                            amask_e.v[:, i * 128:(i + 1) * 128],
                                            start=(i == 0), stop=False,
                                            skip_group_check=True)
                            last = t.matmul(reg, vo.v[:, dc * 128:(dc + 1) * 128],
                                            amask_o.v[:, i * 128:(i + 1) * 128],
                                            start=False, stop=first,
                                            skip_group_check=True)
                            if not first:
                                last = t.matmul(reg, kvp.v[:, dc * 128:(dc + 1) * 128],
                                                fmq.v[:, dc * GA + ci * 128: dc * GA + (ci + 1) * 128],
                                                start=False, stop=True,
                                                skip_group_check=True)
                        return last
                    nm = p.emit("tensor", numw, fnum, inc=pe)
                    nbk.produced(nm)
                    amask_e.consumed(pe, nm[1]); amask_o.consumed(pe, nm[1])
                wlast = None
                for kb in range(2):
                    bkk = bres[4 + kb]
                    km = p.emit("tensor", fmk_tm.r() + bkk.rw(), lambda t, kb=kb, bkk=bkk: [
                        t.matmul(bkk.v[0:128, i * 128:(i + 1) * 128],
                                 fmk_tm.v[:, (kb * 4 + i) * 128:(kb * 4 + i + 1) * 128],
                                 v_tril[:, 0:128], start=True, stop=True)
                        for i in range(4)][-1], inc=pe)
                    bkk.produced(km)
                    for i in range(4):
                        fc = kb * 4 + i
                        wm = p.emit("vector", bkk.r() + w16.rw() + [zmark],
                                    lambda v, kb=kb, i=i, fc=fc, bkk=bkk:
                                    v.tensor_scalar_add(
                                        out=w16.v[:, fc * 128:(fc + 1) * 128],
                                        in0=bkk.v[0:128, i * 128:(i + 1) * 128],
                                        scalar1=v_z[:, fc:fc + 1]), inc=dv)
                        wlast = wm
                    bkk.consumed(dv, wlast[1])
                w16.produced(wlast)
                pm = p.emit("vector", w16.r() + fmq.r() + p16.rw(), lambda v, ci=ci:
                            v.tensor_mul(
                                out=p16.v.rearrange("p (b x) -> p b x", x=128),
                                in0=fmq.v.rearrange("p (b x) -> p b x", x=GA)[:, :, ci * 128:(ci + 1) * 128],
                                in1=w16.v.rearrange("p (b x) -> p b x", x=128)), inc=dv)
                p16.produced(pm)
                w16.consumed(dv, pm[1])
                rdl = None
                for kb in range(2):
                    bkd = bres[2 + kb]
                    dm = p.emit("tensor", p16.r() + bkd.rw(), lambda t, kb=kb, bkd=bkd: [
                        t.matmul(bkd.v[0:128, i * 128:(i + 1) * 128], v_epad,
                                 p16.v[:, (kb * 4 + i) * 128:(kb * 4 + i + 1) * 128],
                                 start=True, stop=True) for i in range(4)][-1], inc=pe)
                    bkd.produced(dm)
                    r1 = p.emit("vector", bkd.r() + amask_e.rw(),
                                lambda v, bkd=bkd:
                                v.tensor_scalar_add(out=amask_e.v[0:2, 0:512],
                                                    in0=bkd.v[0:2, 0:512],
                                                    scalar1=ATT_EPS), inc=dv)
                    amask_e.produced(r1)
                    bkd.consumed(dv, r1[1])
                    def frecip(v, kb=kb):
                        with nc.allow_low_precision(reason="rd is bf16 by design"):
                            return v.reciprocal(out=rd16.v[0:2, kb * 512:(kb + 1) * 512],
                                                in_=amask_e.v[0:2, 0:512])
                    r2 = p.emit("vector", [r1] + rd16.rw(), frecip, inc=dv)
                    amask_e.consumed(dv, r2[1])
                    rdl = r2
                rd16.produced(rdl)
                r_rows.produced(rdl)
                olast = None
                for kb in range(2):
                    bkr = bres[4 + kb]
                    rm = p.emit("tensor", rd16.r() + bkr.rw(), lambda t, kb=kb, bkr=bkr:
                                t.matmul(bkr.v[0:128, 0:512], v_e2pad[0:128, :],
                                         rd16.v[0:128, kb * 512:(kb + 1) * 512],
                                         start=True, stop=True), inc=pe)
                    bkr.produced(rm)
                    re = p.emit("scalar", bkr.r() + (rdrep.rw() if kb == 0 else []),
                                lambda s, kb=kb, bkr=bkr:
                                s.copy(out=rdrep.v[:, kb * 512:(kb + 1) * 512],
                                       in_=bkr.v[0:128, 0:512]), inc=ac)
                    bkr.consumed(ac, re[1])
                    olast = re
                rdrep.produced(olast)
                olist = []
                for rnd in range(2):
                    nbk = num_banks[rnd]
                    om = p.emit("vector", nbk.r() + rdrep.r() + (osb.rw() if (ci == CPG - 1 and rnd == 0) else []),
                                lambda v, ci=ci, rnd=rnd, nbk=nbk:
                        v.tensor_mul(
                            out=osb.v.rearrange("p (b x) -> p b x", x=GA)[:, rnd * 4:(rnd + 1) * 4, ci * 128:(ci + 1) * 128],
                            in0=nbk.v[0:128, 0:512].rearrange("p (b x) -> p b x", x=128),
                            in1=rdrep.v[:, rnd * 512:(rnd + 1) * 512]
                                .rearrange("p (b x) -> p b x", x=128)), inc=dv)
                    nbk.consumed(dv, om[1])
                    olist.append(om)
                rdrep.consumed(dv, olist[-1][1])
                kvw = (fmk_tm.r() + ve.r() + vo.r() +
                       merge_waits(bKV[0].frees + bKV[1].frees))
                def fkv(t, first=first):
                    last = None
                    for dc in range(NFC):
                        kb = dc // 4
                        reg = bKV[kb].v[0:128, (dc % 4) * 128:((dc % 4) + 1) * 128]
                        last = t.matmul(reg, fmk_tm.v[:, dc * 128:(dc + 1) * 128],
                                        ve.v[:, dc * 128:(dc + 1) * 128],
                                        start=(first and dc % 4 == 0), stop=False,
                                        skip_group_check=True)
                        last = t.matmul(reg, fmk_tm.v[:, dc * 128:(dc + 1) * 128],
                                        vo.v[:, dc * 128:(dc + 1) * 128],
                                        start=False, stop=False, skip_group_check=True)
                    return last
                kvm = p.emit("tensor", kvw, fkv, inc=pe)
                bKV[0].produced(kvm); bKV[1].produced(kvm)
                ve.consumed(pe, kvm[1]); vo.consumed(pe, kvm[1])
                fmk_tm.consumed(pe, kvm[1])
                bz = bres[2]
                zm = p.emit("tensor", fmk_tm.r() + bz.rw(), lambda t: [
                    t.matmul(bz.v[0:128, fc:fc + 1], fmk_tm.v[:, fc * 128:(fc + 1) * 128],
                             v_onescol[:, 0:1], start=True, stop=True)
                    for fc in range(NFC)][-1], inc=pe)
                bz.produced(zm)
                zup = p.emit("vector", bz.r() + [zmark], lambda v:
                             v.tensor_add(out=v_z, in0=v_z, in1=bz.v[0:128, 0:8]), inc=dv)
                bz.consumed(dv, zup[1])
                zmark = zup
                r_z.produced(zup)
            osb.produced((dv, dv.n))
            rlast = None
            for dc in range(NFC):
                bkp = bres[2 + dc % 2]
                mm = p.emit("tensor", osb.r() + r_wo.r() + bkp.rw(),
                            lambda t, dc=dc, bkp=bkp: [
                    t.matmul(bkp.v[0:128, 0:GA],
                             wo16[:, kc * 1024 + dc * 128: kc * 1024 + (dc + 1) * 128],
                             osb.v[:, kc * GA:(kc + 1) * GA],
                             start=(kc == 0), stop=(kc == NFC - 1))
                    for kc in range(NFC)][-1], inc=pe)
                bkp.produced(mm)
                if dc == NFC - 1:
                    osb.consumed(pe, mm[1])
                rm = p.emit("vector", bkp.r() + bx.r() + r_bo.r() +
                            (r32.rw() if dc == 0 else []),
                            lambda v, dc=dc, bkp=bkp, bx=bx:
                            v.scalar_tensor_tensor(
                                out=r32.v[:, dc * GA:(dc + 1) * GA],
                                in0=bkp.v[0:128, 0:GA], scalar=v_bo[:, dc:dc + 1],
                                in1=bx.v[:, dc * GA:(dc + 1) * GA],
                                op0=OP.add, op1=OP.add), inc=dv)
                bkp.consumed(dv, rm[1])
                rlast = rm
            r32.produced(rlast)
            bx.consumed(dv, rlast[1])
            t1scr.frees = list(w16.ready) + list(w16.frees)
            lnm = ln_feature_major(r32, GA, xg16, fmq, bres[4], bres[5],
                                   (bres[4], bres[6]), (bres[5], bres[7]), t1scr)
            w16.consumed(dv, lnm[1])
            store_fm(Y32, t0, GA, r32)
        p.q["gpsimd"].append(lambda e, t=dS.n: e.wait_ge(dS.h, t))

    def stage_ffn(l):
        w1sb = sv(0, 128, 8 * 4096, BF16)
        w2sb = sv(65536, 128, 32 * 1024, BF16)
        r_w1 = Res(w1sb, "w1"); r_w2 = Res(w2sb, "w2")
        w1m = load_weight(W1A, l, D, DFF, w1sb, r_w1, jmajor=True)
        w2m = load_weight(W2A, l, DFF, D, w2sb, r_w2)
        load_small(b1tA[l], v_b1, r_b1)
        load_small(b2tA[l], v_b2, r_b2)
        prep_ln_rows(g2rA, be2rA, l)
        DST = XS[l % 2]
        yg = [Res(sv(131072 + i * 8192, 128, NFC * GB, F32), f"yg{i}") for i in range(2)]
        y16 = Res(sv(147456, 128, NFC * GB, BF16), "y16")
        h16 = Res(sv(151552, 128, 32 * GB, BF16), "h16")
        t1scr = Res(sv(167936, 128, GB, F32), "t1b")
        sq16 = Res(sv(151552, 128, NFC * GB, BF16), "sq16b")
        for g in range(NGB):
            t0 = g * GB
            by = yg[g % 2]
            dma_load(by, lambda e, t0=t0, by=by:
                     e.dma_start(out=fmv(by.v), in_=fm_slice(Y32, t0, GB)))
            cst = p.emit("scalar", by.r() + y16.rw(), lambda s, by=by:
                         s.copy(out=y16.v, in_=by.v), inc=ac)
            y16.produced(cst)
            glast = None
            for hc in range(32):
                bkp = bres[hc % 2]
                w1w = ([w1m[(NFC - 1, hc // 8)]] if g == 0 else [])
                mm = p.emit("tensor", y16.r() + w1w + (r_w1.frees if g == 0 else []) + bkp.rw(),
                            lambda t, hc=hc, bkp=bkp: [
                    t.matmul(bkp.v[0:128, 0:GB],
                             w1sb[:, kc * 4096 + hc * 128: kc * 4096 + (hc + 1) * 128],
                             y16.v[:, kc * GB:(kc + 1) * GB],
                             start=(kc == 0), stop=(kc == NFC - 1))
                    for kc in range(NFC)][-1], inc=pe)
                bkp.produced(mm)
                gm = p.emit("scalar", bkp.r() + r_b1.r() + (h16.rw() if hc == 0 else []),
                            lambda s, hc=hc, bkp=bkp:
                            s.activation(out=h16.v[:, hc * GB:(hc + 1) * GB],
                                         in_=bkp.v[0:128, 0:GB], func=AF.Gelu,
                                         bias=v_b1[:, hc:hc + 1], scale=1.0), inc=ac)
                bkp.consumed(ac, gm[1])
                glast = gm
            h16.produced(glast)
            rlast = None
            for half in range(2):
                for i in range(4):
                    dc = half * 4 + i
                    bkp = bres[2 + i]
                    mm = None
                    for hq in range(4):
                        w2w = ([w2m[(hq * 8 + 7, 0)]] if g == 0 else [])
                        mm = p.emit("tensor", (h16.r() + bkp.rw() + w2w +
                                               (r_w2.frees if g == 0 else [])) if hq == 0
                                    else ([mm] + w2w),
                                    lambda t, dc=dc, bkp=bkp, hq=hq: [
                            t.matmul(bkp.v[0:128, 0:GB],
                                     w2sb[:, hc * 1024 + dc * 128: hc * 1024 + (dc + 1) * 128],
                                     h16.v[:, hc * GB:(hc + 1) * GB],
                                     start=(hc == 0), stop=(hc == 31),
                                     skip_group_check=True)
                            for hc in range(hq * 8, hq * 8 + 8)][-1], inc=pe)
                    bkp.produced(mm)
                    if dc == NFC - 1:
                        h16.consumed(pe, mm[1])
                    rm = p.emit("vector", bkp.r() + by.r() + r_b2.r(),
                                lambda v, dc=dc, bkp=bkp, by=by:
                                v.scalar_tensor_tensor(
                                    out=by.v[:, dc * GB:(dc + 1) * GB],
                                    in0=bkp.v[0:128, 0:GB], scalar=v_b2[:, dc:dc + 1],
                                    in1=by.v[:, dc * GB:(dc + 1) * GB],
                                    op0=OP.add, op1=OP.add), inc=dv)
                    bkp.consumed(dv, rm[1])
                    rlast = rm
            by.produced(rlast)
            sq16.frees = list(h16.ready) + list(h16.frees)
            lnm = ln_feature_major(by, GB, y16, sq16, bres[6], bres[7],
                                   (bres[6], bres[2]), (bres[7], bres[3]), t1scr)
            h16.consumed(ac, p.ac.n)
            h16.consumed(pe, p.pe.n)
            store_fm(DST, t0, GB, by)
        p.q["gpsimd"].append(lambda e, t=dS.n: e.wait_ge(dS.h, t))

    def stage_fln():
        SRCF = XS[(NLAYER - 1) % 2]
        prep_ln_rows({0: gfr}, {0: bfr}, 0)
        T = GB
        xg = [Res(sv(i * 8192, 128, NFC * T, F32), f"fx{i}") for i in range(2)]
        rb16 = Res(sv(16384, 128, NFC * T, BF16), "frb")
        sq16 = Res(sv(20480, 128, NFC * T, BF16), "fsq")
        s16 = Res(sv(24576, 128, NFC * T, BF16), "fs16")
        t1scr = Res(sv(28672, 128, T, F32), "ft1")
        for g in range(NGB):
            t0 = g * T
            bx = xg[g % 2]
            dma_load(bx, lambda e, t0=t0, bx=bx:
                     e.dma_start(out=fmv(bx.v), in_=fm_slice(SRCF, t0, T)))
            ln_feature_major(bx, T, rb16, sq16, bres[6], bres[7],
                             (bres[6], bres[2]), (bres[7], bres[3]), t1scr)
            cst = p.emit("scalar", bx.r() + s16.rw(), lambda s, bx=bx:
                         s.copy(out=s16.v, in_=bx.v), inc=ac)
            s16.produced(cst)
            bx.consumed(ac, cst[1])
            store_fm(S16, t0, T, s16)
        p.q["gpsimd"].append(lambda e, t=dS.n: e.wait_ge(dS.h, t))

    def stage_moe():
        w1sb = sv(0, 128, 8 * 4096, BF16)
        w2sb = sv(65536, 128, 32 * 1024, BF16)
        r_w1 = Res(w1sb, "mw1"); r_w2 = Res(w2sb, "mw2")
        xn = [Res(sv(131072 + i * 4096, 128, NFC * GB, BF16), f"mx{i}") for i in range(2)]
        acc = [Res(sv(139264 + i * 4096, 128, NFC * GB, BF16), f"ma{i}") for i in range(2)]
        h16 = Res(sv(151552, 128, 32 * GB, BF16), "mh16")
        grep = Res(sv(147456, 128, GB, BF16), "grep")
        gpad = Res(sv(147968, 128, GB, BF16), "gpad")
        growc = Res(sv(148480, 1, GB, F32), "growc")
        tscr = Res(sv(149504, 128, GB, BF16), "mtscr")
        zi = p.emit("vector", gpad.rw(), lambda v: v.memset(gpad.v, 0.0), inc=dv)
        gpad.produced(zi)
        for e_i in range(NE):
            w1m = load_weight(mw1A, e_i, D, DFF, w1sb, r_w1, jmajor=True)
            w2m = load_weight(mw2A, e_i, DFF, D, w2sb, r_w2)
            load_small(mb1tA[e_i], v_b1, r_b1)
            load_small(mb2tA[e_i], v_b2, r_b2)
            for g in range(NGB):
                t0 = g * GB
                bx = xn[g % 2]
                ba = acc[g % 2]
                dma_load(bx, lambda e, t0=t0, bx=bx:
                         e.dma_start(out=fmv(bx.v), in_=fm_slice(XN16, t0, GB)))
                if e_i > 0:
                    dma_load(ba, lambda e, t0=t0, ba=ba:
                             e.dma_start(out=fmv(ba.v), in_=fm_slice(ACC, t0, GB)))
                gld = load_small(GT[e_i:e_i + 1, t0:t0 + GB], growc.v, growc)
                gc = p.emit("vector", growc.r() + gpad.rw(), lambda v:
                            v.tensor_copy(out=gpad.v[0:1, :], in_=growc.v), inc=dv)
                gpad.produced(gc)
                growc.consumed(dv, gc[1])
                bk6 = bres[6]
                gm = p.emit("tensor", gpad.r() + bk6.rw(), lambda t:
                            t.matmul(bk6.v[0:128, 0:GB], v_onesrow[0:128, :],
                                     gpad.v[0:128, :], start=True, stop=True), inc=pe)
                bk6.produced(gm)
                gpad.consumed(pe, gm[1])
                ge = p.emit("scalar", bk6.r() + grep.rw(), lambda s:
                            s.copy(out=grep.v, in_=bk6.v[0:128, 0:GB]), inc=ac)
                bk6.consumed(ac, ge[1])
                grep.produced(ge)
                glast = None
                for hc in range(32):
                    bkp = bres[hc % 2]
                    w1w = ([w1m[(NFC - 1, hc // 8)]] if g == 0 else [])
                    mm = p.emit("tensor", bx.r() + w1w + (r_w1.frees if g == 0 else []) + bkp.rw(),
                                lambda t, hc=hc, bkp=bkp, bx=bx: [
                        t.matmul(bkp.v[0:128, 0:GB],
                                 w1sb[:, kc * 4096 + hc * 128: kc * 4096 + (hc + 1) * 128],
                                 bx.v[:, kc * GB:(kc + 1) * GB],
                                 start=(kc == 0), stop=(kc == NFC - 1))
                        for kc in range(NFC)][-1], inc=pe)
                    bkp.produced(mm)
                    if hc == 31:
                        bx.consumed(pe, mm[1])
                    hm = p.emit("scalar", bkp.r() + r_b1.r() + (h16.rw() if hc == 0 else []),
                                lambda s, hc=hc, bkp=bkp:
                                s.activation(out=h16.v[:, hc * GB:(hc + 1) * GB],
                                             in_=bkp.v[0:128, 0:GB], func=AF.Gelu,
                                             bias=v_b1[:, hc:hc + 1], scale=1.0), inc=ac)
                    bkp.consumed(ac, hm[1])
                    glast = hm
                h16.produced(glast)
                alast = None
                for half in range(2):
                    for i in range(4):
                        dc = half * 4 + i
                        bkp = bres[2 + i]
                        mm = None
                        for hq in range(4):
                            w2w = ([w2m[(hq * 8 + 7, 0)]] if g == 0 else [])
                            mm = p.emit("tensor", (h16.r() + bkp.rw() + w2w +
                                                   (r_w2.frees if g == 0 else [])) if hq == 0
                                        else ([mm] + w2w),
                                        lambda t, dc=dc, bkp=bkp, hq=hq: [
                                t.matmul(bkp.v[0:128, 0:GB],
                                         w2sb[:, hc * 1024 + dc * 128: hc * 1024 + (dc + 1) * 128],
                                         h16.v[:, hc * GB:(hc + 1) * GB],
                                         start=(hc == 0), stop=(hc == 31),
                                         skip_group_check=True)
                                for hc in range(hq * 8, hq * 8 + 8)][-1], inc=pe)
                        bkp.produced(mm)
                        if dc == NFC - 1:
                            h16.consumed(pe, mm[1])
                        if e_i == 0:
                            am = p.emit("vector", bkp.r() + grep.r() + r_b2.r() +
                                        (ba.rw() if dc == 0 else []),
                                        lambda v, dc=dc, bkp=bkp, ba=ba:
                                        v.scalar_tensor_tensor(
                                            out=ba.v[:, dc * GB:(dc + 1) * GB],
                                            in0=bkp.v[0:128, 0:GB],
                                            scalar=v_b2[:, dc:dc + 1],
                                            in1=grep.v, op0=OP.add, op1=OP.mult), inc=dv)
                            bkp.consumed(dv, am[1])
                        else:
                            t1 = p.emit("vector", bkp.r() + grep.r() + r_b2.r() + tscr.rw(),
                                        lambda v, dc=dc, bkp=bkp:
                                        v.scalar_tensor_tensor(
                                            out=tscr.v, in0=bkp.v[0:128, 0:GB],
                                            scalar=v_b2[:, dc:dc + 1],
                                            in1=grep.v, op0=OP.add, op1=OP.mult), inc=dv)
                            tscr.produced(t1)
                            bkp.consumed(dv, t1[1])
                            am = p.emit("vector", tscr.r() + ba.r(),
                                        lambda v, dc=dc, ba=ba:
                                        v.tensor_add(out=ba.v[:, dc * GB:(dc + 1) * GB],
                                                     in0=ba.v[:, dc * GB:(dc + 1) * GB],
                                                     in1=tscr.v), inc=dv)
                            tscr.consumed(dv, am[1])
                        alast = am
                ba.produced(alast)
                grep.consumed(dv, alast[1])
                store_fm(ACC, t0, GB, ba)
            r_b1.consumed(ac, ac.n)
            r_b2.consumed(dv, dv.n)
            r_w1.consumed(pe, pe.n)
            r_w2.consumed(pe, pe.n)
        p.q["gpsimd"].append(lambda e, t=dS.n: e.wait_ge(dS.h, t))

    def stage_final():
        xb = [Res(sv(i * 4096, 128, 1024, F32), f"fxb{i}") for i in range(2)]
        sb = [Res(sv(8192 + i * 2048, 128, 1024, BF16), f"fsb{i}") for i in range(2)]
        ab = [Res(sv(12288 + i * 2048, 128, 1024, BF16), f"fab{i}") for i in range(2)]
        stm = Res(sv(16384, 128, 1024, BF16), "stm")
        gsl = Res(sv(20480, 128, 1024, F32), "gsl")
        ob = [Res(sv(24576 + i * 4096, 128, 1024, F32), f"fob{i}") for i in range(2)]
        for i in range(NCHUNK):
            bx, bs, ba, bo_ = xb[i % 2], sb[i % 2], ab[i % 2], ob[i % 2]
            dma_load(bx, lambda e, i=i, bx=bx:
                     e.dma_start(out=bx.v, in_=x_in[i * 128:(i + 1) * 128, :]))
            dma_load(bs, lambda e, i=i, bs=bs:
                     e.dma_start(out=fmv(bs.v), in_=fm_slice(S16, i * 128, 128)))
            dma_load(ba, lambda e, i=i, ba=ba:
                     e.dma_start(out=fmv(ba.v), in_=fm_slice(ACC, i * 128, 128)))
            b4, b5 = bres[4], bres[5]
            tr1 = p.emit("tensor", bs.r() + b4.rw(), lambda t, bs=bs: [
                t.transpose(banks16[4][0:128, c * 128:(c + 1) * 128],
                            bs.v[:, c * 128:(c + 1) * 128], v_id16)
                for c in range(NFC)][-1], inc=pe)
            b4.produced(tr1)
            bs.consumed(pe, tr1[1])
            tr2 = p.emit("tensor", ba.r() + b5.rw(), lambda t, ba=ba: [
                t.transpose(banks16[5][0:128, c * 128:(c + 1) * 128],
                            ba.v[:, c * 128:(c + 1) * 128], v_id16)
                for c in range(NFC)][-1], inc=pe)
            b5.produced(tr2)
            ba.consumed(pe, tr2[1])
            ev1 = p.emit("vector", b4.r() + stm.rw(), lambda v:
                         v.tensor_copy(out=stm.v, in_=banks16[4][0:128, 0:1024]), inc=dv)
            b4.consumed(dv, ev1[1])
            stm.produced(ev1)
            ev2 = p.emit("scalar", b5.r() + gsl.rw(), lambda s:
                         s.activation(out=gsl.v, in_=banks16[5][0:128, 0:1024],
                                      func=AF.Silu, bias=0.0, scale=1.0), inc=ac)
            b5.consumed(ac, ev2[1])
            gsl.produced(ev2)
            o1 = p.emit("vector", stm.r() + gsl.r() + bo_.rw(), lambda v, bo_=bo_:
                        v.tensor_mul(out=bo_.v, in0=stm.v, in1=gsl.v), inc=dv)
            stm.consumed(dv, o1[1]); gsl.consumed(dv, o1[1])
            o2 = p.emit("vector", [o1] + bx.r(), lambda v, bo_=bo_, bx=bx:
                        v.tensor_add(out=bo_.v, in0=bo_.v, in1=bx.v), inc=dv)
            bx.consumed(dv, o2[1])
            bo_.produced(o2)
            store(lambda i=i: out_t[i * 128:(i + 1) * 128, :], bo_.v, bo_.r(),
                  consumers=(bo_,))
        p.q["gpsimd"].append(lambda e, t=dS.n: e.wait_ge(dS.h, t))

    order = ["ln0", "gate", "attn", "moe", "final"]
    stage_ln0()
    p.barrier()
    stage_gate()
    p.barrier()
    if upto in ("attn", "moe", "final"):
        nl = NLAYER if upto != "attn1" else 1
        for l in range(NLAYER):
            stage_attn(l)
            p.barrier()
            stage_ffn(l)
            p.barrier()
        stage_fln()
        p.barrier()
    if upto in ("moe", "final"):
        stage_moe()
        p.barrier()
    if upto == "final":
        stage_final()
    else:
        def stage_copyout():
            bufs = [Res(sv(i * 4096, 128, 1024, F32), f"co{i}") for i in range(2)]
            for i in range(NCHUNK):
                bxx = bufs[i % 2]
                dma_load(bxx, lambda e, i=i, bxx=bxx:
                         e.dma_start(out=bxx.v, in_=x_in[i * 128:(i + 1) * 128, :]))
                store(lambda i=i: out_t[i * 128:(i + 1) * 128, :], bxx.v, bxx.r(),
                      consumers=(bxx,))
            p.q["gpsimd"].append(lambda e, t=dS.n: e.wait_ge(dS.h, t))
        stage_copyout()

    p.realize()
    return nc, din


def make_consts():
    tril = np.tril(np.ones((128, 128), np.float32))
    c = {}
    c["c_idf"] = np.eye(128, dtype=np.float32)
    c["c_tril16"] = np.tile(tril, (1, 4)).astype(ml_dtypes.bfloat16)
    c["c_id16"] = np.eye(128, dtype=ml_dtypes.bfloat16)
    ep = np.zeros((128, 128), np.float32); ep[0:64, 0] = 1; ep[64:128, 1] = 1
    c["c_epad"] = ep.astype(ml_dtypes.bfloat16)
    e2 = np.zeros((128, 128), np.float32); e2[0, 0:64] = 1; e2[1, 64:128] = 1
    c["c_e2pad"] = e2.astype(ml_dtypes.bfloat16)
    oc = np.zeros((128, 128), np.float32); oc[:, 0] = 1
    c["c_onescol"] = oc.astype(ml_dtypes.bfloat16)
    orw = np.zeros((128, 128), np.float32); orw[0, :] = 1
    c["c_onesrow"] = orw.astype(ml_dtypes.bfloat16)
    return c


def prep_inputs(inp, nseq, batch_idx):
    g = lambda k: np.asarray(inp[k])
    m = {}
    m["x"] = np.ascontiguousarray(g("L0_attn_x")[batch_idx, :nseq])
    m["Wq"] = g("L0_attn_Wq"); m["Wk"] = g("L0_attn_Wk")
    m["Wv"] = g("L0_attn_Wv"); m["Wo"] = g("L0_attn_Wo")
    t8 = lambda a: np.ascontiguousarray(a.reshape(-1, 8, 128).transpose(0, 2, 1))
    m["bq8"] = t8(g("L0_attn_bq")); m["bk8"] = t8(g("L0_attn_bk"))
    m["bo8"] = t8(g("L0_attn_bo"))
    m["bvrow"] = np.ascontiguousarray(g("L0_attn_bv").reshape(NLAYER, 1, D))
    m["W1"] = g("L0_attn_W1"); m["W2"] = g("L0_attn_W2")
    m["b1t"] = np.ascontiguousarray(g("L0_attn_b1").reshape(NLAYER, 32, 128).transpose(0, 2, 1))
    m["b2t"] = t8(g("L0_attn_b2"))
    r1 = lambda a: np.ascontiguousarray(a.reshape(-1, 1, D))
    m["g1r"] = r1(g("L0_attn_g1")); m["be1r"] = r1(g("L0_attn_be1"))
    m["g2r"] = r1(g("L0_attn_g2")); m["be2r"] = r1(g("L0_attn_be2"))
    m["gfr"] = np.ascontiguousarray(g("L0_attn_gf").reshape(1, D))
    m["bfr"] = np.ascontiguousarray(g("L0_attn_bf").reshape(1, D))
    m["mw1"] = g("L0_ffn_w1"); m["mw2"] = g("L0_ffn_w2")
    m["mb1t"] = np.ascontiguousarray(g("L0_ffn_b1").reshape(NE, 32, 128).transpose(0, 2, 1))
    m["mb2t"] = np.ascontiguousarray(g("L0_ffn_b2").reshape(NE, 8, 128).transpose(0, 2, 1))
    wr = np.zeros((128, 8, 128), np.float32)
    wr[:, :, 0:16] = g("L0_ffn_wg_red").reshape(16, 8, 128).transpose(2, 1, 0)
    m["wgredT"] = wr.reshape(128, 1024)
    m["wg"] = g("L0_ffn_wg")
    m.update(make_consts())
    return {k: np.ascontiguousarray(v) for k, v in m.items()}


NSEQ = 8192
_CACHE = {}


def kernel(**inputs):
    if "nc" not in _CACHE:
        _CACHE["nc"], _ = build(NSEQ)
    nc = _CACHE["nc"]
    in_maps = [prep_inputs(inputs, NSEQ, b) for b in range(8)]
    res = run_bass_kernel_spmd(nc, in_maps, core_ids=list(range(8)))
    return np.stack([np.asarray(res.results[b]["out"]) for b in range(8)], axis=0)


# version 03 (final)
# speedup vs baseline: 1.0946x; 1.0946x over previous
import contextlib
import numpy as np
import ml_dtypes
import concourse.bass as bass
import concourse.mybir as mybir
from concourse.bass_utils import run_bass_kernel_spmd

F32 = mybir.dt.float32
BF16 = mybir.dt.bfloat16
AF = mybir.ActivationFunctionType
OP = mybir.AluOpType
AX = mybir.AxisListType.X

D = 1024
H = 16
DH = 64
NLAYER = 6
DFF = 4096
NE = 8
CH = 128
NFC = D // 128
LN_EPS = 1e-5
ATT_EPS = 1e-6
GATE_EPS = 1e-4

ENGINES = ("sync", "tensor", "vector", "scalar", "gpsimd")


class Sem:
    def __init__(self, nc, stack, name):
        self.h = stack.enter_context(nc.semaphore(name))
        self.n = 0
        self.name = name
    def bump(self, k=1):
        self.n += k
        return self.n


def merge_waits(waits):
    d = {}
    for s, t in waits:
        if t is None:
            continue
        d[s] = max(d.get(s, 0), t)
    return [(s, t) for s, t in d.items() if t > 0]


class Res:
    def __init__(self, view=None, name=""):
        self.v = view
        self.name = name
        self.ready = []
        self.frees = []
    def produced(self, *deps):
        self.ready = list(deps)
        self.frees = []
    def consumed(self, sem, t):
        self.frees.append((sem, t))
    def rw(self):
        return merge_waits(self.ready + self.frees)
    def r(self):
        return merge_waits(self.ready)


class P:
    def __init__(self, nc, stack):
        self.nc = nc
        self.q = {e: [] for e in ENGINES}
        self.dA = Sem(nc, stack, "dA")
        self.dW = Sem(nc, stack, "dW")
        self.dWp = [Sem(nc, stack, "dW0"), Sem(nc, stack, "dW1")]
        self.dS = Sem(nc, stack, "dS")
        self.pe = Sem(nc, stack, "pe")
        self.gp = Sem(nc, stack, "gp")
        self.dv = Sem(nc, stack, "dv")
        self.ac = Sem(nc, stack, "ac")
        self.bar = Sem(nc, stack, "bar")

    def emit(self, eng, waits, fn, inc=None, inck=1):
        w = merge_waits(waits)
        t = inc.bump(inck) if inc is not None else None
        def op(e, w=w, fn=fn, inc=inc, inck=inck):
            for s, c in w:
                e.wait_ge(s.h, c)
            last = fn(e)
            if inc is not None:
                assert last is not None, "fn must return last instruction"
                last.then_inc(inc.h, inck)
        self.q[eng].append(op)
        return (inc, t) if inc is not None else None

    def seq(self, eng, waits, steps, inc):
        mark = None
        for i, fn in enumerate(steps):
            mark = self.emit(eng, (waits if i == 0 else [mark]), fn, inc=inc)
        return mark

    def barrier(self):
        self.bar.bump(5)
        tgt = self.bar.n
        for eng in ENGINES:
            def op(e, tgt=tgt):
                e.nop().then_inc(self.bar.h, 1)
                e.wait_ge(self.bar.h, tgt)
            self.q[eng].append(op)

    def realize(self):
        with self.nc.Block() as block:
            for eng in ENGINES:
                ops = self.q[eng]
                def body(e, ops=ops):
                    for f in ops:
                        f(e)
                getattr(block, eng)(body)


class Arena:
    def __init__(self, nc, stack, name, kib):
        self.bytes = kib * 1024
        self.t = stack.enter_context(nc.sbuf_tensor(name, [128, self.bytes // 2], BF16))
        self.h32 = self.t.bitcast(F32)
        self.h16 = self.t
    def v(self, off, parts, elems, dtype, p0=0):
        esz = 4 if dtype == F32 else 2
        assert off % esz == 0 and off + elems * esz <= self.bytes, (off, elems, esz)
        h = self.h32 if dtype == F32 else self.h16
        return h[p0:p0 + parts, off // esz: off // esz + elems]


def build(nseq, debug_taps=(), upto="final"):
    NCHUNK = nseq // CH
    GA = min(512, nseq)
    NGA = nseq // GA
    CPG = GA // CH
    GB = min(256, nseq)
    NGB = nseq // GB

    stack = contextlib.ExitStack()
    nc = bass.Bass("TRN2", target_bir_lowering=False)
    p = P(nc, stack)
    dA, dW, dS, pe, dv, ac = p.dA, p.dW, p.dS, p.pe, p.dv, p.ac

    din = {}
    def dtin(name, shape, dtype=F32):
        din[name] = nc.dram_tensor(name, list(shape), dtype, kind="ExternalInput").ap()
        return din[name]
    x_in = dtin("x", (nseq, D))
    WqA = dtin("Wq", (NLAYER, D, D)); WkA = dtin("Wk", (NLAYER, D, D))
    WvA = dtin("Wv", (NLAYER, D, D)); WoA = dtin("Wo", (NLAYER, D, D))
    bq8A = dtin("bq8", (NLAYER, 128, 8)); bk8A = dtin("bk8", (NLAYER, 128, 8))
    bo8A = dtin("bo8", (NLAYER, 128, 8))
    bvrowA = dtin("bvrow", (NLAYER, 1, D))
    W1A = dtin("W1", (NLAYER, D, DFF)); W2A = dtin("W2", (NLAYER, DFF, D))
    b1tA = dtin("b1t", (NLAYER, 128, 32)); b2tA = dtin("b2t", (NLAYER, 128, 8))
    g1rA = dtin("g1r", (NLAYER, 1, D)); be1rA = dtin("be1r", (NLAYER, 1, D))
    g2rA = dtin("g2r", (NLAYER, 1, D)); be2rA = dtin("be2r", (NLAYER, 1, D))
    gfr = dtin("gfr", (1, D)); bfr = dtin("bfr", (1, D))
    mw1A = dtin("mw1", (NE, D, DFF)); mw2A = dtin("mw2", (NE, DFF, D))
    mb1tA = dtin("mb1t", (NE, 128, 32)); mb2tA = dtin("mb2t", (NE, 128, 8))
    wgredT_in = dtin("wgredT", (128, 1024))
    wg = dtin("wg", (8, 16))
    c_idf = dtin("c_idf", (128, 128))
    c_tril16 = dtin("c_tril16", (128, 512), BF16)
    c_id16 = dtin("c_id16", (128, 128), BF16)
    c_epad = dtin("c_epad", (128, 128), BF16)
    c_e2pad = dtin("c_e2pad", (128, 128), BF16)
    c_onescol = dtin("c_onescol", (128, 128), BF16)
    c_onesrow = dtin("c_onesrow", (128, 128), BF16)

    out_t = nc.dram_tensor("out", [nseq, D], F32, kind="ExternalOutput").ap()

    def dscratch(name, shape, dtype):
        kind = "ExternalOutput" if name in debug_taps else "Internal"
        return nc.dram_tensor(name, list(shape), dtype, kind=kind).ap()
    XN32 = dscratch("XN32", (D, nseq), F32)
    XN16 = dscratch("XN16", (D, nseq), BF16)
    XS = [dscratch("XA", (D, nseq), F32), dscratch("XB", (D, nseq), F32)]
    Y32 = dscratch("Y32", (D, nseq), F32)
    S16 = dscratch("S16", (D, nseq), BF16)
    ACC = dscratch("ACC", (D, nseq), BF16)
    GT = dscratch("GT", (8, nseq), F32)

    AR = Arena(nc, stack, "AR", 206)
    v_idf = AR.v(0, 128, 128, F32)
    v_tril = AR.v(512, 128, 512, BF16)
    v_id16 = AR.v(1536, 128, 128, BF16)
    v_epad = AR.v(1792, 128, 128, BF16)
    v_e2pad = AR.v(2048, 128, 128, BF16)
    v_onescol = AR.v(2304, 128, 128, BF16)
    v_onesrow = AR.v(2560, 128, 128, BF16)
    v_wgu = AR.v(2816, 128, 128, F32)
    v_eps = AR.v(3840, 128, 4, F32)
    stg = [AR.v(4096, 128, 1024, F32), AR.v(8192, 128, 1024, F32)]
    stgr = [Res(stg[0], "stg0"), Res(stg[1], "stg1")]
    SM = 12288
    v_bq = AR.v(SM + 0, 128, 8, F32);  v_bqn = AR.v(SM + 32, 128, 8, F32)
    v_bk = AR.v(SM + 64, 128, 8, F32); v_bkn = AR.v(SM + 96, 128, 8, F32)
    v_bo = AR.v(SM + 128, 128, 8, F32)
    v_b2 = AR.v(SM + 160, 128, 8, F32)
    v_b1 = AR.v(SM + 192, 128, 32, F32)
    v_z = AR.v(SM + 320, 128, 8, F32)
    v_gax = AR.v(SM + 512, 128, 1024, BF16)
    v_gbx = AR.v(SM + 2560, 128, 1024, BF16)
    v_abA = AR.v(SM + 4608, 128, 512, BF16)
    v_abB = AR.v(SM + 5632, 128, 512, BF16)
    v_bvrep = AR.v(SM + 6656, 128, 1024, BF16)
    v_wgredT = AR.v(SM + 8704, 128, 1024, F32)
    v_wguT = AR.v(SM + 12800, 128, 128, F32)
    v_grow = AR.v(SM + 13312, 1, 1024, F32)
    v_berow = AR.v(SM + 17408, 1, 1024, F32)
    v_rowM = AR.v(SM + 21504, 1, 512, F32)
    v_rowV = AR.v(SM + 23552, 1, 512, F32)
    v_rowR = AR.v(SM + 25600, 1, 512, F32)
    SM_END = SM + 27648
    STAGE = 40960
    STAGE_SZ = 206 * 1024 - STAGE
    assert SM_END <= STAGE
    def sv(off, parts, elems, dtype, p0=0):
        esz = 4 if dtype == F32 else 2
        assert 0 <= off and off + elems * esz <= STAGE_SZ, (off, elems)
        return AR.v(STAGE + off, parts, elems, dtype, p0=p0)

    banks = [stack.enter_context(nc.psum_tensor(f"pb{i}", [128, 512], F32)) for i in range(8)]
    banks16 = [b.bitcast(BF16) for b in banks]
    bres = [Res(banks[i], f"bank{i}") for i in range(8)]

    r_gax = Res(v_gax, "gax"); r_gbx = Res(v_gbx, "gbx")
    r_abA = Res(v_abA, "abA"); r_abB = Res(v_abB, "abB")
    r_bvrep = Res(v_bvrep, "bvrep")
    r_bq = Res(v_bq, "bq"); r_bqn = Res(v_bqn, "bqn")
    r_bk = Res(v_bk, "bk"); r_bkn = Res(v_bkn, "bkn")
    r_bo = Res(v_bo, "bo"); r_b1 = Res(v_b1, "b1"); r_b2 = Res(v_b2, "b2")
    r_z = Res(v_z, "z")
    r_grow = Res(v_grow, "grow"); r_berow = Res(v_berow, "berow")
    r_rows = Res(None, "rows")

    def dma_load(dst_res, fn_dma, extra=()):
        w = merge_waits(list(dst_res.frees) + list(extra) + [(dA, dA.n)])
        t = dA.bump(16)
        def op(e, w=w, fn=fn_dma, t=t):
            for s, c in w:
                e.wait_ge(s.h, c)
            fn(e).then_inc(dA.h, 16)
        p.q["sync"].append(op)
        dst_res.produced((dA, t))
        return (dA, t)

    def fm_slice(dram, t0, T):
        return dram.rearrange("(c p) n -> p c n", p=128)[:, :, t0:t0 + T]

    def fmv(view):
        return view.rearrange("p (c t) -> p c t", c=NFC)

    def store(dram_ap_fn, src_view, waits, consumers=()):
        w = merge_waits(list(waits) + [(dS, dS.n)])
        t = dS.bump(16)
        def op(g, w=w, t=t):
            for s, c in w:
                g.wait_ge(s.h, c)
            g.dma_start(out=dram_ap_fn(), in_=src_view).then_inc(dS.h, 16)
        p.q["gpsimd"].append(op)
        for r in consumers:
            r.consumed(dS, t)
        return (dS, t)

    def store_fm(dram, t0, T, src_res):
        return store(lambda dram=dram, t0=t0, T=T: fm_slice(dram, t0, T),
                     src_res.v.rearrange("p (c t) -> p c t", c=NFC),
                     src_res.r(), consumers=(src_res,))

    wload = {"phase": 0}
    def load_weight(dram3, lidx, rows, cols, dest16, dest_res, jmajor=False):
        marks = {}
        order = ([(r, j) for j in range(cols // 1024) for r in range(rows // 128)]
                 if jmajor else
                 [(r, j) for r in range(rows // 128) for j in range(cols // 1024)])
        for (r, j) in order:
            if True:
                half = wload["phase"] % 2
                wload["phase"] += 1
                sres = stgr[half]
                dWh = p.dWp[half]
                w = merge_waits(sres.frees + dest_res.frees)
                t = dWh.bump(16)
                def opd(s, r=r, j=j, half=half, w=w, t=t, dram3=dram3, lidx=lidx,
                        cols=cols, dWh=dWh):
                    for sm, c in w:
                        s.wait_ge(sm.h, c)
                    src = dram3[lidx, r * 128:(r + 1) * 128, j * 1024:(j + 1) * 1024]
                    s.dma_start(out=stg[half], in_=src).then_inc(dWh.h, 16)
                p.q["scalar"].append(opd)
                sres.produced((dWh, t))
                cast = p.emit("scalar", sres.r(),
                              lambda e, half=half, r=r, j=j, cols=cols, dest16=dest16:
                              e.copy(out=dest16[:, r * cols + j * 1024:r * cols + (j + 1) * 1024],
                                     in_=stg[half]),
                              inc=ac)
                sres.consumed(ac, cast[1])
                marks[(r, j)] = cast
        dest_res.produced((ac, ac.n))
        return marks

    def load_small(dram2d, dst_view, dst_res=None):
        w = merge_waits((dst_res.frees if dst_res else []) + [(dA, dA.n)])
        t = dA.bump(16)
        def op(e, w=w, t=t):
            for s, c in w:
                e.wait_ge(s.h, c)
            e.dma_start(out=dst_view, in_=dram2d).then_inc(dA.h, 16)
        p.q["sync"].append(op)
        if dst_res is not None:
            dst_res.produced((dA, t))
        return (dA, t)

    def fm3(view, e):
        pass

    def f_consts(sync):
        for vv, src in [(v_idf, c_idf), (v_tril, c_tril16), (v_id16, c_id16),
                        (v_epad, c_epad), (v_e2pad, c_e2pad), (v_onescol, c_onescol),
                        (v_onesrow, c_onesrow), (v_wgredT, wgredT_in)]:
            sync.dma_start(out=vv, in_=src).then_inc(dA.h, 16)
    p.q["sync"].append(f_consts)
    dA.bump(8 * 16)
    consts_t = dA.n
    for eng in ("tensor", "vector", "scalar", "gpsimd"):
        p.q[eng].append(lambda e, t=consts_t: e.wait_ge(dA.h, t))
    zinit = p.seq("vector", [], [
        lambda v: (v.memset(v_eps[:, 0:1], LN_EPS),
                   v.memset(v_eps[:, 1:2], ATT_EPS),
                   v.memset(v_gax, 0.0),
                   v.memset(v_gbx, 0.0),
                   v.memset(v_abA, 0.0),
                   v.memset(v_abB, 0.0),
                   v.memset(v_wguT, 0.0))[-1],
        lambda v: (v.memset(v_abB[32:33, :], 1.0),
                   v.memset(v_abB[64:65, :], 1.0))[-1],
    ], dv)
    p.barrier()

    def stage_ln0():
        bufs = [Res(sv(i * 4096, 128, 1024, F32), f"ln0x{i}") for i in range(2)]
        r_x32 = [Res(sv(8192 + i * 4096, 128, 1024, F32), f"xn32_{i}") for i in range(2)]
        r_x16 = [Res(sv(16384 + i * 2048, 128, 1024, BF16), f"xn16_{i}") for i in range(2)]
        stats = [Res(sv(20480 + i * 128, 128, 16, F32), f"st{i}") for i in range(2)]
        for i in range(NCHUNK):
            bx = bufs[i % 2]
            x32, x16 = r_x32[i % 2], r_x16[i % 2]
            st = stats[i % 2]
            sview = st.v
            dma_load(bx, lambda e, i=i, bx=bx:
                     e.dma_start(out=bx.v, in_=x_in[i * 128:(i + 1) * 128, :]))
            s1 = p.emit("vector", bx.r() + st.rw(), lambda v, bx=bx, sview=sview: (
                v.bn_stats(sview[:, 0:6], bx.v[:, 0:512]),
                v.bn_stats(sview[:, 6:12], bx.v[:, 512:1024]))[-1], inc=dv)
            s2 = p.emit("vector", [s1], lambda v, sview=sview:
                        v.bn_aggr(sview[:, 12:14], sview[:, 0:12]), inc=dv)
            s3 = p.emit("scalar", [s2], lambda s, sview=sview:
                        s.activation(out=sview[:, 14:15], in_=sview[:, 13:14],
                                     func=AF.Sqrt, bias=v_eps[:, 0:1], scale=1.0), inc=ac)
            s4 = p.emit("vector", [s3], lambda v, sview=sview:
                        v.reciprocal(out=sview[:, 14:15], in_=sview[:, 14:15]), inc=dv)
            s5 = p.emit("vector", [s4], lambda v, bx=bx, sview=sview:
                        v.tensor_scalar(out=bx.v, in0=bx.v, scalar1=sview[:, 12:13],
                                        scalar2=sview[:, 14:15], op0=OP.subtract,
                                        op1=OP.mult), inc=dv)
            st.produced(s5)
            lastev = None
            for half in range(2):
                bk = bres[half]
                trm = p.emit("tensor", [s5] + bk.rw(),
                             lambda t, bx=bx, half=half, bk=bk: [
                    t.transpose(bk.v[0:128, c * 128:(c + 1) * 128],
                                bx.v[:, (half * 4 + c) * 128:(half * 4 + c + 1) * 128],
                                v_idf) for c in range(4)][-1], inc=pe)
                bk.produced(trm)
                ev = p.emit("vector", bk.r() + (x32.rw() + x16.rw() if half == 0 else []),
                            lambda v, half=half, bk=bk, x32=x32, x16=x16: (
                    v.tensor_copy(out=x32.v[:, half * 512:(half + 1) * 512], in_=bk.v[:, 0:512]),
                    v.tensor_copy(out=x16.v[:, half * 512:(half + 1) * 512],
                                  in_=bk.v[:, 0:512]))[-1], inc=dv)
                bk.consumed(dv, ev[1])
                bx.consumed(pe, trm[1])
                lastev = ev
            x32.produced(lastev); x16.produced(lastev)
            store_fm(XN32, i * 128, 128, x32)
            store_fm(XN16, i * 128, 128, x16)
        p.q["gpsimd"].append(lambda e, t=dS.n: e.wait_ge(dS.h, t))

    def stage_gate():
        wgr = Res(None, "wg")
        load_small(wg, v_wgu[0:8, 0:16], wgr)
        pm = p.seq("vector", wgr.r(), [
            lambda v: v.tensor_tensor(out=v_wgu[0:8, 16:32], in0=v_wgu[0:8, 0:16],
                                      in1=v_wgu[0:8, 0:16], op=OP.mult),
            lambda v: v.reduce_sum(out=v_wgu[0:8, 32:33], in_=v_wgu[0:8, 16:32], axis=AX),
        ], dv)
        pm2 = p.emit("scalar", [pm], lambda s:
                     s.activation(out=v_wgu[0:8, 33:34], in_=v_wgu[0:8, 32:33],
                                  func=AF.Sqrt, bias=0.0, scale=1.0), inc=ac)
        pm3 = p.seq("vector", [pm2], [
            lambda v: v.tensor_scalar_max(out=v_wgu[0:8, 33:34], in0=v_wgu[0:8, 33:34],
                                          scalar1=GATE_EPS),
            lambda v: v.reciprocal(out=v_wgu[0:8, 34:35], in_=v_wgu[0:8, 33:34]),
            lambda v: v.tensor_scalar_mul(out=v_wgu[0:8, 48:64], in0=v_wgu[0:8, 0:16],
                                          scalar1=v_wgu[0:8, 34:35]),
        ], dv)
        tr = p.emit("tensor", [pm3] + bres[7].rw(), lambda t:
                    t.transpose(banks[7][0:16, 0:8], v_wgu[0:8, 48:64], v_idf[0:8, 0:8]),
                    inc=pe)
        bres[7].produced(tr)
        pm4 = p.emit("vector", bres[7].r(), lambda v:
                     v.tensor_copy(out=v_wguT[0:16, 0:8], in_=banks[7][0:16, 0:8]), inc=dv)
        bres[7].consumed(dv, pm4[1])
        wgu_ok = pm4

        T = GA
        xbuf = [Res(sv(i * 16384, 128, NFC * T, F32), f"gx{i}") for i in range(2)]
        lgrow = [Res(sv(32768 + i * 2048, 8, T, F32), f"lgr{i}") for i in range(2)]
        xrr = [Res(sv(36864 + i * 2048, 128, T, F32), f"xr{i}") for i in range(2)]
        gtr = [Res(sv(40960 + i * 2048, 8, T, F32), f"gtb{i}") for i in range(2)]
        v_mx = sv(45056, 128, 8, F32)
        v_mxn = sv(45088, 128, 8, F32)
        v_sum = sv(45120, 128, 8, F32)
        v_sc = sv(45152, 128, 8, F32)
        r_eq = Res(sv(45184, 128, 8, F32), "eq")
        r_ex = Res(sv(45216, 128, 8, F32), "ex")
        zi = p.emit("vector", [x.rw() for x in xrr][0] + xrr[1].rw(), lambda v: (
            v.memset(xrr[0].v, 0.0), v.memset(xrr[1].v, 0.0))[-1], inc=dv)
        for xr in xrr:
            xr.consumed(dv, zi[1])
        for g in range(nseq // T):
            bx = xbuf[g % 2]
            dma_load(bx, lambda e, g=g, bx=bx, T=T:
                     e.dma_start(out=fmv(bx.v), in_=fm_slice(XN32, g * T, T)))
            bk = bres[0]
            mm1 = p.emit("tensor", bx.r() + bk.rw(), lambda t, bx=bx, T=T: [
                t.matmul(banks[0][0:128, 0:T], v_wgredT[:, c * 128:(c + 1) * 128],
                         bx.v[:, c * T:(c + 1) * T], start=(c == 0), stop=(c == NFC - 1))
                for c in range(NFC)][-1], inc=pe)
            bk.produced(mm1)
            bx.consumed(pe, mm1[1])
            xr = xrr[g % 2]
            ev1 = p.emit("vector", bk.r() + xr.rw(), lambda v, xr=xr, T=T:
                         v.tensor_copy(out=xr.v[0:16, 0:T], in_=banks[0][0:16, 0:T]), inc=dv)
            bk.consumed(dv, ev1[1])
            xr.produced(ev1)
            bk1 = bres[1]
            mm2 = p.emit("tensor", xr.r() + bk1.rw() + [wgu_ok], lambda t, xr=xr, T=T:
                         t.matmul(banks[1][0:128, 0:T], v_wguT[0:128, 0:128],
                                  xr.v[0:128, 0:T], start=True, stop=True), inc=pe)
            bk1.produced(mm2)
            xr.consumed(pe, mm2[1])
            lg = lgrow[g % 2]
            ev2 = p.emit("vector", bk1.r() + lg.rw(), lambda v, lg=lg, T=T:
                         v.tensor_copy(out=lg.v, in_=banks[1][0:8, 0:T]), inc=dv)
            bk1.consumed(dv, ev2[1])
            lg.produced(ev2)
            gt = gtr[g % 2]
            last_evb = None
            for tc in range(T // 128):
                bk2 = bres[2 + tc % 2]
                trm = p.emit("tensor", lg.r() + bk2.rw(), lambda t, lg=lg, tc=tc, bk2=bk2:
                             t.transpose(bk2.v[0:128, 0:8],
                                         lg.v[:, tc * 128:(tc + 1) * 128],
                                         v_idf[0:8, 0:8]), inc=pe)
                bk2.produced(trm)
                if tc == T // 128 - 1:
                    lg.consumed(pe, trm[1])
                sm1 = p.seq("vector", bk2.r(), [
                    lambda v, bk2=bk2, tc=tc:
                        v.reduce_max(out=v_mx[:, tc:tc + 1], in_=bk2.v[0:128, 0:8], axis=AX),
                    lambda v, tc=tc:
                        v.tensor_scalar_mul(out=v_mxn[:, tc:tc + 1], in0=v_mx[:, tc:tc + 1],
                                            scalar1=-1.0)], dv)
                sm2 = p.emit("vector", [sm1] + r_eq.rw(), lambda v, bk2=bk2, tc=tc:
                             v.tensor_scalar(out=r_eq.v, in0=bk2.v[0:128, 0:8],
                                             scalar1=v_mx[:, tc:tc + 1], scalar2=None,
                                             op0=OP.is_equal), inc=dv)
                sm3 = p.emit("scalar", [sm1] + r_ex.rw(), lambda s, bk2=bk2, tc=tc:
                             s.activation(out=r_ex.v, in_=bk2.v[0:128, 0:8], func=AF.Exp,
                                          bias=v_mxn[:, tc:tc + 1], scale=1.0), inc=ac)
                bk2.consumed(ac, sm3[1]); bk2.consumed(dv, sm2[1])
                r_ex.produced(sm3)
                sm6 = p.seq("vector", r_ex.r() + [sm2], [
                    lambda v, tc=tc: v.reduce_sum(out=v_sum[:, tc:tc + 1], in_=r_ex.v, axis=AX),
                    lambda v, tc=tc: v.reciprocal(out=v_sc[:, tc:tc + 1], in_=v_sum[:, tc:tc + 1]),
                    lambda v, tc=tc: v.tensor_scalar_mul(out=r_eq.v, in0=r_eq.v,
                                                         scalar1=v_sc[:, tc:tc + 1])], dv)
                r_ex.consumed(dv, sm6[1])
                r_eq.produced(sm6)
                bk3 = bres[4 + tc % 2]
                trb = p.emit("tensor", r_eq.r() + bk3.rw(), lambda t, bk3=bk3:
                             t.transpose(bk3.v[0:8, 0:128], r_eq.v, v_idf), inc=pe)
                bk3.produced(trb)
                r_eq.consumed(pe, trb[1])
                evb = p.emit("vector", bk3.r() + (gt.rw() if tc == 0 else []),
                             lambda v, bk3=bk3, gt=gt, tc=tc:
                             v.tensor_copy(out=gt.v[:, tc * 128:(tc + 1) * 128],
                                           in_=bk3.v[0:8, 0:128]), inc=dv)
                bk3.consumed(dv, evb[1])
                last_evb = evb
            gt.produced(last_evb)
            store(lambda g=g, T=T: GT[:, g * T:(g + 1) * T], gt.v, gt.r(), consumers=(gt,))
        p.q["gpsimd"].append(lambda e, t=dS.n: e.wait_ge(dS.h, t))

    def prep_ln_rows(growd, berowd, lidx):
        load_small(growd[lidx], v_grow, r_grow)
        load_small(berowd[lidx], v_berow, r_berow)
        m1 = p.seq("vector", r_grow.r() + r_gax.rw() + r_gbx.rw() + r_abA.rw(), [
            lambda v: v.tensor_copy(out=v_gax[0:1, :], in_=v_grow),
            lambda v: (v.tensor_copy(out=v_gax[32:33, :], in_=v_gax[0:1, :]),
                       v.tensor_copy(out=v_gbx[0:1, :], in_=v_gax[0:1, :]),
                       v.tensor_sub(out=v_abA[0:1, 0:512], in0=v_grow[0:1, 0:512],
                                    in1=v_gax[0:1, 0:512]),
                       v.tensor_sub(out=v_abA[0:1, 0:512] if False else v_abB[0:1, 0:512],
                                    in0=v_grow[0:1, 512:1024],
                                    in1=v_gax[0:1, 512:1024]))[-1],
            lambda v: (v.tensor_copy(out=v_gax[64:65, 0:512], in_=v_abA[0:1, 0:512]),
                       v.tensor_copy(out=v_gax[64:65, 512:1024], in_=v_abB[0:1, 0:512]))[-1],
        ], dv)
        m2 = p.seq("vector", r_berow.r() + r_abB.rw() + [m1], [
            lambda v: v.tensor_copy(out=v_gbx[32:33, :], in_=v_berow),
            lambda v: (v.tensor_sub(out=v_abA[0:1, 0:512], in0=v_berow[0:1, 0:512],
                                    in1=v_gbx[0:1, 0:512] if False else v_gbx[0:1, 0:512]),)[-1]
                      if False else
                      (v.tensor_copy(out=v_abA[0:1, 0:512], in_=v_berow[0:1, 0:512]),
                       v.tensor_copy(out=v_abB[0:1, 0:512], in_=v_berow[0:1, 512:1024]))[-1],
            lambda v: (v.tensor_sub(out=v_abA[0:1, 0:512], in0=v_berow[0:1, 0:512],
                                    in1=v_abA[0:1, 0:512]),
                       v.tensor_sub(out=v_abB[0:1, 0:512], in0=v_berow[0:1, 512:1024],
                                    in1=v_abB[0:1, 0:512]))[-1],
            lambda v: (v.tensor_copy(out=v_gbx[64:65, 0:512], in_=v_abA[0:1, 0:512]),
                       v.tensor_copy(out=v_gbx[64:65, 512:1024], in_=v_abB[0:1, 0:512]))[-1],
        ], dv)
        r_abA.consumed(dv, m2[1]); r_abB.consumed(dv, m2[1])
        r_gax.produced(m1); r_gbx.produced(m2)
        r_grow.consumed(dv, m1[1]); r_berow.consumed(dv, m2[1])
        return merge_waits([m1, m2])

    def ln_feature_major(r32res, T, rb16, sq16, bk_sums, bk_sumsq, bk_repA, bk_repB,
                         t1scr, extra_rows=None):
        c1 = p.emit("scalar", r32res.r() + rb16.rw(), lambda s:
                    s.copy(out=rb16.v, in_=r32res.v), inc=ac)
        rb16.produced(c1)
        c2 = p.emit("scalar", r32res.r() + sq16.rw(), lambda s: (
            s.square(out=sq16.v, in_=r32res.v)), inc=ac)
        sq16.produced(c2)
        ms = p.emit("tensor", rb16.r() + bk_sums.rw(), lambda t: [
            t.matmul(bk_sums.v[0:128, 0:T], v_onescol, rb16.v[:, c * T:(c + 1) * T],
                     start=(c == 0), stop=(c == NFC - 1)) for c in range(NFC)][-1], inc=pe)
        bk_sums.produced(ms)
        rb16.consumed(pe, ms[1])
        mq = p.emit("tensor", sq16.r() + bk_sumsq.rw(), lambda t: [
            t.matmul(bk_sumsq.v[0:128, 0:T], v_onescol, sq16.v[:, c * T:(c + 1) * T],
                     start=(c == 0), stop=(c == NFC - 1)) for c in range(NFC)][-1], inc=pe)
        bk_sumsq.produced(mq)
        sq16.consumed(pe, mq[1])
        d1 = p.emit("vector", bk_sums.r() + r_rows.rw(), lambda v:
                    v.tensor_scalar_mul(out=v_rowM[:, 0:T], in0=bk_sums.v[0:1, 0:T],
                                        scalar1=1.0 / D), inc=dv)
        bk_sums.consumed(dv, d1[1])
        d2 = p.emit("vector", [d1], lambda v:
                    v.tensor_mul(out=v_rowV[:, 0:T], in0=v_rowM[:, 0:T],
                                 in1=v_rowM[:, 0:T]), inc=dv)
        d3 = p.emit("vector", [d2] + bk_sumsq.r(), lambda v:
                    v.scalar_tensor_tensor(out=v_rowV[:, 0:T], in0=bk_sumsq.v[0:1, 0:T],
                                           scalar=1.0 / D, in1=v_rowV[:, 0:T],
                                           op0=OP.mult, op1=OP.subtract), inc=dv)
        bk_sumsq.consumed(dv, d3[1])
        d4 = p.emit("scalar", [d3], lambda s:
                    s.activation(out=v_rowR[:, 0:T], in_=v_rowV[:, 0:T], func=AF.Sqrt,
                                 bias=v_eps[0:1, 0:1], scale=1.0), inc=ac)
        d5 = p.emit("vector", [d4], lambda v:
                    v.reciprocal(out=v_rowR[:, 0:T], in_=v_rowR[:, 0:T]), inc=dv)
        d6 = p.emit("vector", [d5] + r_abA.rw(), lambda v:
                    v.tensor_copy(out=v_abA[0:1, 0:T], in_=v_rowR[:, 0:T]), inc=dv)
        d7 = p.emit("vector", [d6], lambda v: (
            v.tensor_sub(out=v_abA[32:33, 0:T], in0=v_rowR[:, 0:T], in1=v_abA[0:1, 0:T]),
            v.tensor_copy(out=v_abA[64:65, 0:T], in_=v_abA[0:1, 0:T]))[-1], inc=dv)
        d8 = p.emit("vector", [d5] + r_abB.rw(), lambda v:
                    v.scalar_tensor_tensor(out=v_abB[0:1, 0:T], in0=v_rowM[:, 0:T],
                                           scalar=-1.0, in1=v_rowR[:, 0:T],
                                           op0=OP.mult, op1=OP.mult), inc=dv)
        r_abA.produced(d7); r_abB.produced(d8)
        r_rows.produced(d7, d8)
        lastap = None
        for c in range(NFC):
            bA = bk_repA[c % 2]
            bB = bk_repB[c % 2]
            mA = p.emit("tensor", r_abA.r() + r_gax.r() + bA.rw(), lambda t, c=c, bA=bA:
                        t.matmul(bA.v[0:128, 0:T], v_gax[:, c * 128:(c + 1) * 128],
                                 v_abA[:, 0:T], start=True, stop=True), inc=pe)
            bA.produced(mA)
            mB = p.emit("tensor", r_abB.r() + r_gbx.r() + bB.rw(), lambda t, c=c, bB=bB:
                        t.matmul(bB.v[0:128, 0:T], v_gbx[:, c * 128:(c + 1) * 128],
                                 v_abB[:, 0:T], start=True, stop=True), inc=pe)
            bB.produced(mB)
            a1 = p.emit("vector", bA.r() + t1scr.rw() + r32res.r(),
                        lambda v, c=c, bA=bA: v.tensor_mul(
                            out=t1scr.v, in0=r32res.v[:, c * T:(c + 1) * T],
                            in1=bA.v[0:128, 0:T]), inc=dv)
            bA.consumed(dv, a1[1])
            t1scr.produced(a1)
            a2 = p.emit("vector", [a1] + bB.r(), lambda v, c=c, bB=bB:
                        v.tensor_add(out=r32res.v[:, c * T:(c + 1) * T], in0=t1scr.v,
                                     in1=bB.v[0:128, 0:T]), inc=dv)
            bB.consumed(dv, a2[1])
            t1scr.consumed(dv, a2[1])
            lastap = a2
        r32res.produced(lastap)
        return lastap

    def fm_emit(src_psum_res, dst16, dst_cols, bias_col, biasneg_col, scr16, T):
        e1 = p.emit("scalar", src_psum_res.r() + scr16.rw(), lambda s:
                    s.activation(out=scr16.v[:, 0:T], in_=src_psum_res.v[0:128, 0:T],
                                 func=AF.Relu, bias=biasneg_col, scale=-1.0), inc=ac)
        scr16.produced(e1)
        e2 = p.emit("scalar", [e1], lambda s:
                    s.activation(out=scr16.v[:, T:2 * T], in_=scr16.v[:, 0:T],
                                 func=AF.Exp, bias=0.0, scale=-1.0), inc=ac)
        e3 = p.emit("scalar", [e2] + src_psum_res.r(), lambda s:
                    s.activation(out=dst16.v[:, dst_cols[0]:dst_cols[1]],
                                 in_=src_psum_res.v[0:128, 0:T],
                                 func=AF.Relu, bias=bias_col, scale=1.0), inc=ac)
        src_psum_res.consumed(ac, e3[1])
        e4 = p.emit("vector", [e3], lambda v:
                    v.tensor_add(out=dst16.v[:, dst_cols[0]:dst_cols[1]],
                                 in0=dst16.v[:, dst_cols[0]:dst_cols[1]],
                                 in1=scr16.v[:, T:2 * T]), inc=dv)
        scr16.consumed(dv, e4[1])
        return e4

    def stage_attn(l):
        SRC = XN32 if l == 0 else XS[(l - 1) % 2]
        wq16 = sv(0, 128, 8 * 1024, BF16); wk16 = sv(16384, 128, 8 * 1024, BF16)
        wv16 = sv(32768, 128, 8 * 1024, BF16); wo16 = sv(49152, 128, 8 * 1024, BF16)
        r_wq = Res(wq16, "wq"); r_wk = Res(wk16, "wk")
        r_wv = Res(wv16, "wv"); r_wo = Res(wo16, "wo")
        load_weight(WqA, l, D, D, wq16, r_wq)
        load_weight(WkA, l, D, D, wk16, r_wk)
        load_weight(WvA, l, D, D, wv16, r_wv)
        load_weight(WoA, l, D, D, wo16, r_wo)
        load_small(bq8A[l], v_bq, r_bq)
        load_small(bk8A[l], v_bk, r_bk)
        load_small(bo8A[l], v_bo, r_bo)
        nb = p.emit("vector", r_bq.r() + r_bk.r() + r_bqn.rw() + r_bkn.rw(), lambda v: (
            v.tensor_scalar_mul(out=v_bqn, in0=v_bq, scalar1=-1.0),
            v.tensor_scalar_mul(out=v_bkn, in0=v_bk, scalar1=-1.0))[-1], inc=dv)
        r_bqn.produced(nb); r_bkn.produced(nb)
        prep_ln_rows(g1rA, be1rA, l)
        bvld = load_small(bvrowA[l], v_grow, r_grow)
        last_bv = None
        for halfc in range(2):
            cp = p.emit("vector", [bvld] + r_abA.rw() + ([last_bv] if last_bv else []),
                        lambda v, halfc=halfc:
                        v.tensor_copy(out=v_abA[0:1, 0:512],
                                      in_=v_grow[0:1, halfc * 512:(halfc + 1) * 512]),
                        inc=dv)
            bkx = bres[6 + halfc]
            mm = p.emit("tensor", [cp] + bkx.rw(), lambda t, bkx=bkx:
                        t.matmul(bkx.v[0:128, 0:512], v_onesrow[0:128, :],
                                 v_abA[0:128, 0:512], start=True, stop=True), inc=pe)
            bkx.produced(mm)
            r_abA.consumed(pe, mm[1])
            ev = p.emit("vector", bkx.r() + (r_bvrep.rw() if halfc == 0 else []),
                        lambda v, halfc=halfc, bkx=bkx:
                        v.tensor_copy(out=v_bvrep[:, halfc * 512:(halfc + 1) * 512],
                                      in_=bkx.v[0:128, 0:512]), inc=dv)
            bkx.consumed(dv, ev[1])
            last_bv = ev
        r_bvrep.produced(last_bv)
        r_grow.consumed(dv, last_bv[1])

        xg = [Res(sv(65536 + i * 16384, 128, NFC * GA, F32), f"xg{i}") for i in range(2)]
        r32 = Res(sv(98304, 128, NFC * GA, F32), "r32")
        xg16 = Res(sv(114688, 128, NFC * GA, BF16), "xg16")
        fmq = Res(sv(122880, 128, NFC * GA, BF16), "fmq")
        fmk = Res(sv(131072, 128, NFC * GA, BF16), "fmk")
        osb = Res(sv(139264, 128, NFC * GA, BF16), "osb")
        ve = Res(sv(147456, 128, 1024, BF16), "ve")
        vo = Res(sv(149504, 128, 1024, BF16), "vo")
        fmk_e = Res(sv(151552, 128, 1024, BF16), "fmk_e")
        fmk_o = Res(sv(153600, 128, 1024, BF16), "fmk_o")
        fmk_tm = Res(sv(155648, 128, 1024, BF16), "fmk_tm")
        p16 = Res(sv(157696, 128, 1024, BF16), "p16")
        rd16 = Res(sv(159744, 128, 1024, BF16), "rd16")
        kvp = Res(sv(161792, 128, 1024, BF16), "kvp")
        w16 = Res(sv(163840, 128, 1024, BF16), "w16")
        rdrep = Res(sv(165888, 128, 1024, BF16), "rdrep")
        fmscr = p16
        t1scr = Res(sv(163840, 128, GA, F32), "t1scr")
        amask_e = Res(sv(167936, 128, 512, BF16), "amask_e")
        amask_o = Res(sv(168960, 128, 512, BF16), "amask_o")
        zl = p.emit("vector", merge_waits(fmk_e.rw() + fmk_o.rw() + kvp.rw()
                                          + rd16.rw() + r_z.rw() + ve.rw() + vo.rw()),
                    lambda v: (
            v.memset(fmk_e.v, 0.0),
            v.memset(fmk_o.v, 0.0),
            v.memset(kvp.v, 0.0),
            v.memset(rd16.v, 0.0),
            v.memset(ve.v, 0.0),
            v.memset(vo.v, 0.0),
            v.memset(v_z, 0.0))[-1], inc=dv)
        for r in (fmk_e, fmk_o, kvp, rd16, ve, vo):
            r.produced(zl)
        zmark = zl
        r_z.produced(zl)

        bKV = [bres[0], bres[1]]

        for g in range(NGA - 1, -1, -1):
            t0 = g * GA
            bx = xg[g % 2]
            dma_load(bx, lambda e, t0=t0, bx=bx:
                     e.dma_start(out=fmv(bx.v), in_=fm_slice(SRC, t0, GA)))
            cst = p.emit("scalar", bx.r() + xg16.rw(), lambda s, bx=bx:
                         s.copy(out=xg16.v, in_=bx.v), inc=ac)
            xg16.produced(cst)
            fm_last = None
            for (wsb, wres, dst, bias, biasn, br, brn) in (
                    (wq16, r_wq, fmq, v_bq, v_bqn, r_bq, r_bqn),
                    (wk16, r_wk, fmk, v_bk, v_bkn, r_bk, r_bkn)):
                for dc in range(NFC):
                    bkp = bres[2 + dc % 2]
                    mm = p.emit("tensor", xg16.r() + wres.r() + bkp.rw() + br.r() + brn.r(),
                                lambda t, wsb=wsb, dc=dc, bkp=bkp: [
                        t.matmul(bkp.v[0:128, 0:GA],
                                 wsb[:, kc * 1024 + dc * 128: kc * 1024 + (dc + 1) * 128],
                                 xg16.v[:, kc * GA:(kc + 1) * GA],
                                 start=(kc == 0), stop=(kc == NFC - 1))
                        for kc in range(NFC)][-1], inc=pe)
                    bkp.produced(mm)
                    fm_last = fm_emit(bkp, dst, (dc * GA, (dc + 1) * GA),
                                      bias[:, dc:dc + 1], biasn[:, dc:dc + 1], fmscr, GA)
            fmq.produced(fm_last); fmk.produced(fm_last)

            for ci in range(CPG - 1, -1, -1):
                n_global = g * CPG + ci
                first = (n_global == NCHUNK - 1)
                ccols = (ci * 128, (ci + 1) * 128)
                vlast = None
                for half in range(2):
                    bkp = bres[4 + half]
                    mm = p.emit("tensor", xg16.r() + r_wv.r() + bkp.rw(),
                                lambda t, ci=ci, half=half, bkp=bkp: [
                        t.matmul(bkp.v[0:128, 0:512],
                                 xg16.v[:, kc * GA + ci * 128: kc * GA + (ci + 1) * 128],
                                 wv16[:, kc * 1024 + half * 512: kc * 1024 + (half + 1) * 512],
                                 start=(kc == 0), stop=(kc == NFC - 1))
                        for kc in range(NFC)][-1], inc=pe)
                    bkp.produced(mm)
                    for par, dstv in ((0, ve), (1, vo)):
                        evm = p.emit("vector", bkp.r() + r_bvrep.r() + dstv.rw(),
                                     lambda v, half=half, par=par, dstv=dstv, bkp=bkp:
                            v.scalar_tensor_tensor(
                                out=dstv.v[:, half * 512:(half + 1) * 512]
                                    .rearrange("p (b x) -> p b x", x=64)[:, par::2, :],
                                in0=bkp.v[0:128, 0:512]
                                    .rearrange("p (b x) -> p b x", x=64)[:, par::2, :],
                                scalar=1.0,
                                in1=v_bvrep[:, half * 512:(half + 1) * 512]
                                    .rearrange("p (b x) -> p b x", x=64)[:, par::2, :],
                                op0=OP.mult, op1=OP.add), inc=dv)
                        vlast = evm
                    bkp.consumed(dv, vlast[1])
                ve.produced(vlast); vo.produced(vlast)
                fsp = p.emit("vector", fmk.r() + fmk_e.rw() + fmk_o.rw(), lambda v, ci=ci: (
                    v.tensor_copy(
                        out=fmk_e.v[0:64, :].rearrange("p (b x) -> p b x", x=128),
                        in_=fmk.v[0:64, :].rearrange("p (b x) -> p b x", x=GA)[:, :, ci * 128:(ci + 1) * 128]),
                    v.tensor_copy(
                        out=fmk_o.v[64:128, :].rearrange("p (b x) -> p b x", x=128),
                        in_=fmk.v[64:128, :].rearrange("p (b x) -> p b x", x=GA)[:, :, ci * 128:(ci + 1) * 128]))[-1],
                    inc=dv)
                fmk_e.produced(fsp); fmk_o.produced(fsp)
                b4, b5 = bres[4], bres[5]
                trm = p.emit("tensor", fmk.r() + b4.rw(), lambda t, ci=ci: [
                    t.transpose(banks16[4][0:128, dc * 128:(dc + 1) * 128],
                                fmk.v[:, dc * GA + ci * 128: dc * GA + (ci + 1) * 128],
                                v_id16) for dc in range(NFC)][-1], inc=pe)
                b4.produced(trm)
                tev = p.emit("vector", b4.r() + fmk_tm.rw(), lambda v:
                             v.tensor_copy(out=fmk_tm.v, in_=banks16[4][0:128, 0:1024]),
                             inc=dv)
                b4.consumed(dv, tev[1])
                fmk_tm.produced(tev)
                if not first:
                    kevs = []
                    for kb in range(2):
                        for base in (0, 64):
                            km = p.emit("vector", bKV[kb].r() + kvp.rw(),
                                        lambda v, kb=kb, base=base:
                                v.tensor_copy(
                                    out=kvp.v[base:base + 64, kb * 512:(kb + 1) * 512]
                                        .rearrange("p (b x) -> p b x", x=128)[:, :, base:base + 64],
                                    in_=bKV[kb].v[base:base + 64, 0:512]
                                        .rearrange("p (b x) -> p b x", x=128)[:, :, base:base + 64]),
                                inc=dv)
                            kevs.append(km)
                            bKV[kb].consumed(dv, km[1])
                    kvp.produced(*kevs[-2:])
                num_banks = (bres[6], bres[7])
                for rnd in range(2):
                    bA, bB = bres[2], bres[3]
                    am = p.emit("tensor", fmk_e.r() + fmk_o.r() + fmq.r() + bA.rw() + bB.rw(),
                                lambda t, ci=ci, rnd=rnd, bA=bA, bB=bB: [
                        t.matmul(bA.v[0:128, i * 128:(i + 1) * 128] if par == 0 else
                                 bB.v[0:128, i * 128:(i + 1) * 128],
                                 (fmk_e if par == 0 else fmk_o).v[:, (rnd * 4 + i) * 128:(rnd * 4 + i + 1) * 128],
                                 fmq.v[:, (rnd * 4 + i) * GA + ci * 128:(rnd * 4 + i) * GA + (ci + 1) * 128],
                                 start=True, stop=True)
                        for i in range(4) for par in range(2)][-1], inc=pe)
                    bA.produced(am); bB.produced(am)
                    me = p.emit("vector", bA.r() + amask_e.rw(), lambda v:
                                v.tensor_mul(out=amask_e.v, in0=bA.v[0:128, 0:512],
                                             in1=v_tril), inc=dv)
                    bA.consumed(dv, me[1])
                    amask_e.produced(me)
                    mo = p.emit("vector", bB.r() + amask_o.rw(), lambda v:
                                v.tensor_mul(out=amask_o.v, in0=bB.v[0:128, 0:512],
                                             in1=v_tril), inc=dv)
                    bB.consumed(dv, mo[1])
                    amask_o.produced(mo)
                    nbk = num_banks[rnd]
                    numw = (amask_e.r() + amask_o.r() + ve.r() + vo.r() + nbk.rw()
                            + (kvp.r() + fmq.r() if not first else []))
                    def fnum(t, ci=ci, rnd=rnd, nbk=nbk, first=first):
                        last = None
                        for i in range(4):
                            dc = rnd * 4 + i
                            reg = nbk.v[0:128, i * 128:(i + 1) * 128]
                            last = t.matmul(reg, ve.v[:, dc * 128:(dc + 1) * 128],
                                            amask_e.v[:, i * 128:(i + 1) * 128],
                                            start=(i == 0), stop=False,
                                            skip_group_check=True)
                            last = t.matmul(reg, vo.v[:, dc * 128:(dc + 1) * 128],
                                            amask_o.v[:, i * 128:(i + 1) * 128],
                                            start=False, stop=first,
                                            skip_group_check=True)
                            if not first:
                                last = t.matmul(reg, kvp.v[:, dc * 128:(dc + 1) * 128],
                                                fmq.v[:, dc * GA + ci * 128: dc * GA + (ci + 1) * 128],
                                                start=False, stop=True,
                                                skip_group_check=True)
                        return last
                    nm = p.emit("tensor", numw, fnum, inc=pe)
                    nbk.produced(nm)
                    amask_e.consumed(pe, nm[1]); amask_o.consumed(pe, nm[1])
                wlast = None
                for kb in range(2):
                    bkk = bres[4 + kb]
                    km = p.emit("tensor", fmk_tm.r() + bkk.rw(), lambda t, kb=kb, bkk=bkk: [
                        t.matmul(bkk.v[0:128, i * 128:(i + 1) * 128],
                                 fmk_tm.v[:, (kb * 4 + i) * 128:(kb * 4 + i + 1) * 128],
                                 v_tril[:, 0:128], start=True, stop=True)
                        for i in range(4)][-1], inc=pe)
                    bkk.produced(km)
                    for i in range(4):
                        fc = kb * 4 + i
                        wm = p.emit("vector", bkk.r() + w16.rw() + [zmark],
                                    lambda v, kb=kb, i=i, fc=fc, bkk=bkk:
                                    v.tensor_scalar_add(
                                        out=w16.v[:, fc * 128:(fc + 1) * 128],
                                        in0=bkk.v[0:128, i * 128:(i + 1) * 128],
                                        scalar1=v_z[:, fc:fc + 1]), inc=dv)
                        wlast = wm
                    bkk.consumed(dv, wlast[1])
                w16.produced(wlast)
                pm = p.emit("vector", w16.r() + fmq.r() + p16.rw(), lambda v, ci=ci:
                            v.tensor_mul(
                                out=p16.v.rearrange("p (b x) -> p b x", x=128),
                                in0=fmq.v.rearrange("p (b x) -> p b x", x=GA)[:, :, ci * 128:(ci + 1) * 128],
                                in1=w16.v.rearrange("p (b x) -> p b x", x=128)), inc=dv)
                p16.produced(pm)
                w16.consumed(dv, pm[1])
                rdl = None
                for kb in range(2):
                    bkd = bres[2 + kb]
                    dm = p.emit("tensor", p16.r() + bkd.rw(), lambda t, kb=kb, bkd=bkd: [
                        t.matmul(bkd.v[0:128, i * 128:(i + 1) * 128], v_epad,
                                 p16.v[:, (kb * 4 + i) * 128:(kb * 4 + i + 1) * 128],
                                 start=True, stop=True) for i in range(4)][-1], inc=pe)
                    bkd.produced(dm)
                    r1 = p.emit("vector", bkd.r() + amask_e.rw(),
                                lambda v, bkd=bkd:
                                v.tensor_scalar_add(out=amask_e.v[0:2, 0:512],
                                                    in0=bkd.v[0:2, 0:512],
                                                    scalar1=ATT_EPS), inc=dv)
                    amask_e.produced(r1)
                    bkd.consumed(dv, r1[1])
                    def frecip(v, kb=kb):
                        with nc.allow_low_precision(reason="rd is bf16 by design"):
                            return v.reciprocal(out=rd16.v[0:2, kb * 512:(kb + 1) * 512],
                                                in_=amask_e.v[0:2, 0:512])
                    r2 = p.emit("vector", [r1] + rd16.rw(), frecip, inc=dv)
                    amask_e.consumed(dv, r2[1])
                    rdl = r2
                rd16.produced(rdl)
                r_rows.produced(rdl)
                olast = None
                for kb in range(2):
                    bkr = bres[4 + kb]
                    rm = p.emit("tensor", rd16.r() + bkr.rw(), lambda t, kb=kb, bkr=bkr:
                                t.matmul(bkr.v[0:128, 0:512], v_e2pad[0:128, :],
                                         rd16.v[0:128, kb * 512:(kb + 1) * 512],
                                         start=True, stop=True), inc=pe)
                    bkr.produced(rm)
                    re = p.emit("scalar", bkr.r() + (rdrep.rw() if kb == 0 else []),
                                lambda s, kb=kb, bkr=bkr:
                                s.copy(out=rdrep.v[:, kb * 512:(kb + 1) * 512],
                                       in_=bkr.v[0:128, 0:512]), inc=ac)
                    bkr.consumed(ac, re[1])
                    olast = re
                rdrep.produced(olast)
                olist = []
                for rnd in range(2):
                    nbk = num_banks[rnd]
                    om = p.emit("vector", nbk.r() + rdrep.r() + (osb.rw() if (ci == CPG - 1 and rnd == 0) else []),
                                lambda v, ci=ci, rnd=rnd, nbk=nbk:
                        v.tensor_mul(
                            out=osb.v.rearrange("p (b x) -> p b x", x=GA)[:, rnd * 4:(rnd + 1) * 4, ci * 128:(ci + 1) * 128],
                            in0=nbk.v[0:128, 0:512].rearrange("p (b x) -> p b x", x=128),
                            in1=rdrep.v[:, rnd * 512:(rnd + 1) * 512]
                                .rearrange("p (b x) -> p b x", x=128)), inc=dv)
                    nbk.consumed(dv, om[1])
                    olist.append(om)
                rdrep.consumed(dv, olist[-1][1])
                kvw = (fmk_tm.r() + ve.r() + vo.r() +
                       merge_waits(bKV[0].frees + bKV[1].frees))
                def fkv(t, first=first):
                    last = None
                    for dc in range(NFC):
                        kb = dc // 4
                        reg = bKV[kb].v[0:128, (dc % 4) * 128:((dc % 4) + 1) * 128]
                        last = t.matmul(reg, fmk_tm.v[:, dc * 128:(dc + 1) * 128],
                                        ve.v[:, dc * 128:(dc + 1) * 128],
                                        start=(first and dc % 4 == 0), stop=False,
                                        skip_group_check=True)
                        last = t.matmul(reg, fmk_tm.v[:, dc * 128:(dc + 1) * 128],
                                        vo.v[:, dc * 128:(dc + 1) * 128],
                                        start=False, stop=False, skip_group_check=True)
                    return last
                kvm = p.emit("tensor", kvw, fkv, inc=pe)
                bKV[0].produced(kvm); bKV[1].produced(kvm)
                ve.consumed(pe, kvm[1]); vo.consumed(pe, kvm[1])
                fmk_tm.consumed(pe, kvm[1])
                bz = bres[2]
                zm = p.emit("tensor", fmk_tm.r() + bz.rw(), lambda t: [
                    t.matmul(bz.v[0:128, fc:fc + 1], fmk_tm.v[:, fc * 128:(fc + 1) * 128],
                             v_onescol[:, 0:1], start=True, stop=True)
                    for fc in range(NFC)][-1], inc=pe)
                bz.produced(zm)
                zup = p.emit("vector", bz.r() + [zmark], lambda v:
                             v.tensor_add(out=v_z, in0=v_z, in1=bz.v[0:128, 0:8]), inc=dv)
                bz.consumed(dv, zup[1])
                zmark = zup
                r_z.produced(zup)
            osb.produced((dv, dv.n))
            rlast = None
            for dc in range(NFC):
                bkp = bres[2 + dc % 2]
                mm = p.emit("tensor", osb.r() + r_wo.r() + bkp.rw(),
                            lambda t, dc=dc, bkp=bkp: [
                    t.matmul(bkp.v[0:128, 0:GA],
                             wo16[:, kc * 1024 + dc * 128: kc * 1024 + (dc + 1) * 128],
                             osb.v[:, kc * GA:(kc + 1) * GA],
                             start=(kc == 0), stop=(kc == NFC - 1))
                    for kc in range(NFC)][-1], inc=pe)
                bkp.produced(mm)
                if dc == NFC - 1:
                    osb.consumed(pe, mm[1])
                rm = p.emit("vector", bkp.r() + bx.r() + r_bo.r() +
                            (r32.rw() if dc == 0 else []),
                            lambda v, dc=dc, bkp=bkp, bx=bx:
                            v.scalar_tensor_tensor(
                                out=r32.v[:, dc * GA:(dc + 1) * GA],
                                in0=bkp.v[0:128, 0:GA], scalar=v_bo[:, dc:dc + 1],
                                in1=bx.v[:, dc * GA:(dc + 1) * GA],
                                op0=OP.add, op1=OP.add), inc=dv)
                bkp.consumed(dv, rm[1])
                rlast = rm
            r32.produced(rlast)
            bx.consumed(dv, rlast[1])
            t1scr.frees = list(w16.ready) + list(w16.frees)
            lnm = ln_feature_major(r32, GA, xg16, fmq, bres[4], bres[5],
                                   (bres[4], bres[6]), (bres[5], bres[7]), t1scr)
            w16.consumed(dv, lnm[1])
            store_fm(Y32, t0, GA, r32)
        p.q["gpsimd"].append(lambda e, t=dS.n: e.wait_ge(dS.h, t))

    def stage_ffn(l):
        w1sb = sv(0, 128, 8 * 4096, BF16)
        w2sb = sv(65536, 128, 32 * 1024, BF16)
        r_w1 = Res(w1sb, "w1"); r_w2 = Res(w2sb, "w2")
        w1m = load_weight(W1A, l, D, DFF, w1sb, r_w1, jmajor=True)
        w2m = load_weight(W2A, l, DFF, D, w2sb, r_w2)
        load_small(b1tA[l], v_b1, r_b1)
        load_small(b2tA[l], v_b2, r_b2)
        prep_ln_rows(g2rA, be2rA, l)
        DST = XS[l % 2]
        yg = [Res(sv(131072 + i * 8192, 128, NFC * GB, F32), f"yg{i}") for i in range(2)]
        y16 = Res(sv(147456, 128, NFC * GB, BF16), "y16")
        h16 = Res(sv(151552, 128, 32 * GB, BF16), "h16")
        t1scr = Res(sv(167936, 128, GB, F32), "t1b")
        sq16 = Res(sv(151552, 128, NFC * GB, BF16), "sq16b")
        for g in range(NGB):
            t0 = g * GB
            by = yg[g % 2]
            dma_load(by, lambda e, t0=t0, by=by:
                     e.dma_start(out=fmv(by.v), in_=fm_slice(Y32, t0, GB)))
            cst = p.emit("scalar", by.r() + y16.rw(), lambda s, by=by:
                         s.copy(out=y16.v, in_=by.v), inc=ac)
            y16.produced(cst)
            glast = None
            for hc in range(32):
                bkp = bres[hc % 2]
                w1w = ([w1m[(NFC - 1, hc // 8)]] if g == 0 else [])
                mm = p.emit("tensor", y16.r() + w1w + (r_w1.frees if g == 0 else []) + bkp.rw(),
                            lambda t, hc=hc, bkp=bkp: [
                    t.matmul(bkp.v[0:128, 0:GB],
                             w1sb[:, kc * 4096 + hc * 128: kc * 4096 + (hc + 1) * 128],
                             y16.v[:, kc * GB:(kc + 1) * GB],
                             start=(kc == 0), stop=(kc == NFC - 1))
                    for kc in range(NFC)][-1], inc=pe)
                bkp.produced(mm)
                gm = p.emit("scalar", bkp.r() + r_b1.r() + (h16.rw() if hc == 0 else []),
                            lambda s, hc=hc, bkp=bkp:
                            s.activation(out=h16.v[:, hc * GB:(hc + 1) * GB],
                                         in_=bkp.v[0:128, 0:GB], func=AF.Gelu,
                                         bias=v_b1[:, hc:hc + 1], scale=1.0), inc=ac)
                bkp.consumed(ac, gm[1])
                glast = gm
            h16.produced(glast)
            rlast = None
            for half in range(2):
                for i in range(4):
                    dc = half * 4 + i
                    bkp = bres[2 + i]
                    if g == 0:
                        mm = None
                        for hq in range(4):
                            w2w = [w2m[(hq * 8 + 7, 0)]]
                            mm = p.emit("tensor", (h16.r() + bkp.rw() + w2w + r_w2.frees)
                                        if hq == 0 else ([mm] + w2w),
                                        lambda t, dc=dc, bkp=bkp, hq=hq: [
                                t.matmul(bkp.v[0:128, 0:GB],
                                         w2sb[:, hc * 1024 + dc * 128: hc * 1024 + (dc + 1) * 128],
                                         h16.v[:, hc * GB:(hc + 1) * GB],
                                         start=(hc == 0), stop=(hc == 31),
                                         skip_group_check=True)
                                for hc in range(hq * 8, hq * 8 + 8)][-1], inc=pe)
                    else:
                        mm = p.emit("tensor", h16.r() + bkp.rw(),
                                    lambda t, dc=dc, bkp=bkp: [
                            t.matmul(bkp.v[0:128, 0:GB],
                                     w2sb[:, hc * 1024 + dc * 128: hc * 1024 + (dc + 1) * 128],
                                     h16.v[:, hc * GB:(hc + 1) * GB],
                                     start=(hc == 0), stop=(hc == 31))
                            for hc in range(32)][-1], inc=pe)
                    bkp.produced(mm)
                    if dc == NFC - 1:
                        h16.consumed(pe, mm[1])
                    rm = p.emit("vector", bkp.r() + by.r() + r_b2.r(),
                                lambda v, dc=dc, bkp=bkp, by=by:
                                v.scalar_tensor_tensor(
                                    out=by.v[:, dc * GB:(dc + 1) * GB],
                                    in0=bkp.v[0:128, 0:GB], scalar=v_b2[:, dc:dc + 1],
                                    in1=by.v[:, dc * GB:(dc + 1) * GB],
                                    op0=OP.add, op1=OP.add), inc=dv)
                    bkp.consumed(dv, rm[1])
                    rlast = rm
            by.produced(rlast)
            sq16.frees = list(h16.ready) + list(h16.frees)
            lnm = ln_feature_major(by, GB, y16, sq16, bres[6], bres[7],
                                   (bres[6], bres[2]), (bres[7], bres[3]), t1scr)
            h16.consumed(ac, p.ac.n)
            h16.consumed(pe, p.pe.n)
            store_fm(DST, t0, GB, by)
        p.q["gpsimd"].append(lambda e, t=dS.n: e.wait_ge(dS.h, t))

    def stage_fln():
        SRCF = XS[(NLAYER - 1) % 2]
        prep_ln_rows({0: gfr}, {0: bfr}, 0)
        T = GB
        xg = [Res(sv(i * 8192, 128, NFC * T, F32), f"fx{i}") for i in range(2)]
        rb16 = Res(sv(16384, 128, NFC * T, BF16), "frb")
        sq16 = Res(sv(20480, 128, NFC * T, BF16), "fsq")
        s16 = Res(sv(24576, 128, NFC * T, BF16), "fs16")
        t1scr = Res(sv(28672, 128, T, F32), "ft1")
        for g in range(NGB):
            t0 = g * T
            bx = xg[g % 2]
            dma_load(bx, lambda e, t0=t0, bx=bx:
                     e.dma_start(out=fmv(bx.v), in_=fm_slice(SRCF, t0, T)))
            ln_feature_major(bx, T, rb16, sq16, bres[6], bres[7],
                             (bres[6], bres[2]), (bres[7], bres[3]), t1scr)
            cst = p.emit("scalar", bx.r() + s16.rw(), lambda s, bx=bx:
                         s.copy(out=s16.v, in_=bx.v), inc=ac)
            s16.produced(cst)
            bx.consumed(ac, cst[1])
            store_fm(S16, t0, T, s16)
        p.q["gpsimd"].append(lambda e, t=dS.n: e.wait_ge(dS.h, t))

    def stage_moe():
        w1sb = sv(0, 128, 8 * 4096, BF16)
        w2sb = sv(65536, 128, 32 * 1024, BF16)
        r_w1 = Res(w1sb, "mw1"); r_w2 = Res(w2sb, "mw2")
        xn = [Res(sv(131072 + i * 4096, 128, NFC * GB, BF16), f"mx{i}") for i in range(2)]
        acc = [Res(sv(139264 + i * 4096, 128, NFC * GB, BF16), f"ma{i}") for i in range(2)]
        h16 = Res(sv(151552, 128, 32 * GB, BF16), "mh16")
        grep = Res(sv(147456, 128, GB, BF16), "grep")
        gpad = Res(sv(147968, 128, GB, BF16), "gpad")
        growc = Res(sv(148480, 1, GB, F32), "growc")
        tscr = Res(sv(149504, 128, GB, BF16), "mtscr")
        zi = p.emit("vector", gpad.rw(), lambda v: v.memset(gpad.v, 0.0), inc=dv)
        gpad.produced(zi)
        for e_i in range(NE):
            w1m = load_weight(mw1A, e_i, D, DFF, w1sb, r_w1, jmajor=True)
            w2m = load_weight(mw2A, e_i, DFF, D, w2sb, r_w2)
            load_small(mb1tA[e_i], v_b1, r_b1)
            load_small(mb2tA[e_i], v_b2, r_b2)
            for g in range(NGB):
                t0 = g * GB
                bx = xn[g % 2]
                ba = acc[g % 2]
                dma_load(bx, lambda e, t0=t0, bx=bx:
                         e.dma_start(out=fmv(bx.v), in_=fm_slice(XN16, t0, GB)))
                if e_i > 0:
                    dma_load(ba, lambda e, t0=t0, ba=ba:
                             e.dma_start(out=fmv(ba.v), in_=fm_slice(ACC, t0, GB)))
                gld = load_small(GT[e_i:e_i + 1, t0:t0 + GB], growc.v, growc)
                gc = p.emit("vector", growc.r() + gpad.rw(), lambda v:
                            v.tensor_copy(out=gpad.v[0:1, :], in_=growc.v), inc=dv)
                gpad.produced(gc)
                growc.consumed(dv, gc[1])
                bk6 = bres[6]
                gm = p.emit("tensor", gpad.r() + bk6.rw(), lambda t:
                            t.matmul(bk6.v[0:128, 0:GB], v_onesrow[0:128, :],
                                     gpad.v[0:128, :], start=True, stop=True), inc=pe)
                bk6.produced(gm)
                gpad.consumed(pe, gm[1])
                ge = p.emit("scalar", bk6.r() + grep.rw(), lambda s:
                            s.copy(out=grep.v, in_=bk6.v[0:128, 0:GB]), inc=ac)
                bk6.consumed(ac, ge[1])
                grep.produced(ge)
                glast = None
                for hc in range(32):
                    bkp = bres[hc % 2]
                    w1w = ([w1m[(NFC - 1, hc // 8)]] if g == 0 else [])
                    mm = p.emit("tensor", bx.r() + w1w + (r_w1.frees if g == 0 else []) + bkp.rw(),
                                lambda t, hc=hc, bkp=bkp, bx=bx: [
                        t.matmul(bkp.v[0:128, 0:GB],
                                 w1sb[:, kc * 4096 + hc * 128: kc * 4096 + (hc + 1) * 128],
                                 bx.v[:, kc * GB:(kc + 1) * GB],
                                 start=(kc == 0), stop=(kc == NFC - 1))
                        for kc in range(NFC)][-1], inc=pe)
                    bkp.produced(mm)
                    if hc == 31:
                        bx.consumed(pe, mm[1])
                    hm = p.emit("scalar", bkp.r() + r_b1.r() + (h16.rw() if hc == 0 else []),
                                lambda s, hc=hc, bkp=bkp:
                                s.activation(out=h16.v[:, hc * GB:(hc + 1) * GB],
                                             in_=bkp.v[0:128, 0:GB], func=AF.Gelu,
                                             bias=v_b1[:, hc:hc + 1], scale=1.0), inc=ac)
                    bkp.consumed(ac, hm[1])
                    glast = hm
                h16.produced(glast)
                alast = None
                for half in range(2):
                    for i in range(4):
                        dc = half * 4 + i
                        bkp = bres[2 + i]
                        if g == 0:
                            mm = None
                            for hq in range(4):
                                w2w = [w2m[(hq * 8 + 7, 0)]]
                                mm = p.emit("tensor", (h16.r() + bkp.rw() + w2w + r_w2.frees)
                                            if hq == 0 else ([mm] + w2w),
                                            lambda t, dc=dc, bkp=bkp, hq=hq: [
                                    t.matmul(bkp.v[0:128, 0:GB],
                                             w2sb[:, hc * 1024 + dc * 128: hc * 1024 + (dc + 1) * 128],
                                             h16.v[:, hc * GB:(hc + 1) * GB],
                                             start=(hc == 0), stop=(hc == 31),
                                             skip_group_check=True)
                                    for hc in range(hq * 8, hq * 8 + 8)][-1], inc=pe)
                        else:
                            mm = p.emit("tensor", h16.r() + bkp.rw(),
                                        lambda t, dc=dc, bkp=bkp: [
                                t.matmul(bkp.v[0:128, 0:GB],
                                         w2sb[:, hc * 1024 + dc * 128: hc * 1024 + (dc + 1) * 128],
                                         h16.v[:, hc * GB:(hc + 1) * GB],
                                         start=(hc == 0), stop=(hc == 31))
                                for hc in range(32)][-1], inc=pe)
                        bkp.produced(mm)
                        if dc == NFC - 1:
                            h16.consumed(pe, mm[1])
                        if e_i == 0:
                            am = p.emit("vector", bkp.r() + grep.r() + r_b2.r() +
                                        (ba.rw() if dc == 0 else []),
                                        lambda v, dc=dc, bkp=bkp, ba=ba:
                                        v.scalar_tensor_tensor(
                                            out=ba.v[:, dc * GB:(dc + 1) * GB],
                                            in0=bkp.v[0:128, 0:GB],
                                            scalar=v_b2[:, dc:dc + 1],
                                            in1=grep.v, op0=OP.add, op1=OP.mult), inc=dv)
                            bkp.consumed(dv, am[1])
                        else:
                            t1 = p.emit("vector", bkp.r() + grep.r() + r_b2.r() + tscr.rw(),
                                        lambda v, dc=dc, bkp=bkp:
                                        v.scalar_tensor_tensor(
                                            out=tscr.v, in0=bkp.v[0:128, 0:GB],
                                            scalar=v_b2[:, dc:dc + 1],
                                            in1=grep.v, op0=OP.add, op1=OP.mult), inc=dv)
                            tscr.produced(t1)
                            bkp.consumed(dv, t1[1])
                            am = p.emit("vector", tscr.r() + ba.r(),
                                        lambda v, dc=dc, ba=ba:
                                        v.tensor_add(out=ba.v[:, dc * GB:(dc + 1) * GB],
                                                     in0=ba.v[:, dc * GB:(dc + 1) * GB],
                                                     in1=tscr.v), inc=dv)
                            tscr.consumed(dv, am[1])
                        alast = am
                ba.produced(alast)
                grep.consumed(dv, alast[1])
                store_fm(ACC, t0, GB, ba)
            r_b1.consumed(ac, ac.n)
            r_b2.consumed(dv, dv.n)
            r_w1.consumed(pe, pe.n)
            r_w2.consumed(pe, pe.n)
        p.q["gpsimd"].append(lambda e, t=dS.n: e.wait_ge(dS.h, t))

    def stage_final():
        xb = [Res(sv(i * 4096, 128, 1024, F32), f"fxb{i}") for i in range(2)]
        sb = [Res(sv(8192 + i * 2048, 128, 1024, BF16), f"fsb{i}") for i in range(2)]
        ab = [Res(sv(12288 + i * 2048, 128, 1024, BF16), f"fab{i}") for i in range(2)]
        stm = Res(sv(16384, 128, 1024, BF16), "stm")
        gsl = Res(sv(20480, 128, 1024, F32), "gsl")
        ob = [Res(sv(24576 + i * 4096, 128, 1024, F32), f"fob{i}") for i in range(2)]
        for i in range(NCHUNK):
            bx, bs, ba, bo_ = xb[i % 2], sb[i % 2], ab[i % 2], ob[i % 2]
            dma_load(bx, lambda e, i=i, bx=bx:
                     e.dma_start(out=bx.v, in_=x_in[i * 128:(i + 1) * 128, :]))
            dma_load(bs, lambda e, i=i, bs=bs:
                     e.dma_start(out=fmv(bs.v), in_=fm_slice(S16, i * 128, 128)))
            dma_load(ba, lambda e, i=i, ba=ba:
                     e.dma_start(out=fmv(ba.v), in_=fm_slice(ACC, i * 128, 128)))
            b4, b5 = bres[4], bres[5]
            tr1 = p.emit("tensor", bs.r() + b4.rw(), lambda t, bs=bs: [
                t.transpose(banks16[4][0:128, c * 128:(c + 1) * 128],
                            bs.v[:, c * 128:(c + 1) * 128], v_id16)
                for c in range(NFC)][-1], inc=pe)
            b4.produced(tr1)
            bs.consumed(pe, tr1[1])
            tr2 = p.emit("tensor", ba.r() + b5.rw(), lambda t, ba=ba: [
                t.transpose(banks16[5][0:128, c * 128:(c + 1) * 128],
                            ba.v[:, c * 128:(c + 1) * 128], v_id16)
                for c in range(NFC)][-1], inc=pe)
            b5.produced(tr2)
            ba.consumed(pe, tr2[1])
            ev1 = p.emit("vector", b4.r() + stm.rw(), lambda v:
                         v.tensor_copy(out=stm.v, in_=banks16[4][0:128, 0:1024]), inc=dv)
            b4.consumed(dv, ev1[1])
            stm.produced(ev1)
            ev2 = p.emit("scalar", b5.r() + gsl.rw(), lambda s:
                         s.activation(out=gsl.v, in_=banks16[5][0:128, 0:1024],
                                      func=AF.Silu, bias=0.0, scale=1.0), inc=ac)
            b5.consumed(ac, ev2[1])
            gsl.produced(ev2)
            o1 = p.emit("vector", stm.r() + gsl.r() + bo_.rw(), lambda v, bo_=bo_:
                        v.tensor_mul(out=bo_.v, in0=stm.v, in1=gsl.v), inc=dv)
            stm.consumed(dv, o1[1]); gsl.consumed(dv, o1[1])
            o2 = p.emit("vector", [o1] + bx.r(), lambda v, bo_=bo_, bx=bx:
                        v.tensor_add(out=bo_.v, in0=bo_.v, in1=bx.v), inc=dv)
            bx.consumed(dv, o2[1])
            bo_.produced(o2)
            store(lambda i=i: out_t[i * 128:(i + 1) * 128, :], bo_.v, bo_.r(),
                  consumers=(bo_,))
        p.q["gpsimd"].append(lambda e, t=dS.n: e.wait_ge(dS.h, t))

    order = ["ln0", "gate", "attn", "moe", "final"]
    stage_ln0()
    p.barrier()
    stage_gate()
    p.barrier()
    if upto in ("attn", "moe", "final"):
        nl = NLAYER if upto != "attn1" else 1
        for l in range(NLAYER):
            stage_attn(l)
            p.barrier()
            stage_ffn(l)
            p.barrier()
        stage_fln()
        p.barrier()
    if upto in ("moe", "final"):
        stage_moe()
        p.barrier()
    if upto == "final":
        stage_final()
    else:
        def stage_copyout():
            bufs = [Res(sv(i * 4096, 128, 1024, F32), f"co{i}") for i in range(2)]
            for i in range(NCHUNK):
                bxx = bufs[i % 2]
                dma_load(bxx, lambda e, i=i, bxx=bxx:
                         e.dma_start(out=bxx.v, in_=x_in[i * 128:(i + 1) * 128, :]))
                store(lambda i=i: out_t[i * 128:(i + 1) * 128, :], bxx.v, bxx.r(),
                      consumers=(bxx,))
            p.q["gpsimd"].append(lambda e, t=dS.n: e.wait_ge(dS.h, t))
        stage_copyout()

    p.realize()
    return nc, din


def make_consts():
    tril = np.tril(np.ones((128, 128), np.float32))
    c = {}
    c["c_idf"] = np.eye(128, dtype=np.float32)
    c["c_tril16"] = np.tile(tril, (1, 4)).astype(ml_dtypes.bfloat16)
    c["c_id16"] = np.eye(128, dtype=ml_dtypes.bfloat16)
    ep = np.zeros((128, 128), np.float32); ep[0:64, 0] = 1; ep[64:128, 1] = 1
    c["c_epad"] = ep.astype(ml_dtypes.bfloat16)
    e2 = np.zeros((128, 128), np.float32); e2[0, 0:64] = 1; e2[1, 64:128] = 1
    c["c_e2pad"] = e2.astype(ml_dtypes.bfloat16)
    oc = np.zeros((128, 128), np.float32); oc[:, 0] = 1
    c["c_onescol"] = oc.astype(ml_dtypes.bfloat16)
    orw = np.zeros((128, 128), np.float32); orw[0, :] = 1
    c["c_onesrow"] = orw.astype(ml_dtypes.bfloat16)
    return c


def prep_inputs(inp, nseq, batch_idx):
    g = lambda k: np.asarray(inp[k])
    m = {}
    m["x"] = np.ascontiguousarray(g("L0_attn_x")[batch_idx, :nseq])
    m["Wq"] = g("L0_attn_Wq"); m["Wk"] = g("L0_attn_Wk")
    m["Wv"] = g("L0_attn_Wv"); m["Wo"] = g("L0_attn_Wo")
    t8 = lambda a: np.ascontiguousarray(a.reshape(-1, 8, 128).transpose(0, 2, 1))
    m["bq8"] = t8(g("L0_attn_bq")); m["bk8"] = t8(g("L0_attn_bk"))
    m["bo8"] = t8(g("L0_attn_bo"))
    m["bvrow"] = np.ascontiguousarray(g("L0_attn_bv").reshape(NLAYER, 1, D))
    m["W1"] = g("L0_attn_W1"); m["W2"] = g("L0_attn_W2")
    m["b1t"] = np.ascontiguousarray(g("L0_attn_b1").reshape(NLAYER, 32, 128).transpose(0, 2, 1))
    m["b2t"] = t8(g("L0_attn_b2"))
    r1 = lambda a: np.ascontiguousarray(a.reshape(-1, 1, D))
    m["g1r"] = r1(g("L0_attn_g1")); m["be1r"] = r1(g("L0_attn_be1"))
    m["g2r"] = r1(g("L0_attn_g2")); m["be2r"] = r1(g("L0_attn_be2"))
    m["gfr"] = np.ascontiguousarray(g("L0_attn_gf").reshape(1, D))
    m["bfr"] = np.ascontiguousarray(g("L0_attn_bf").reshape(1, D))
    m["mw1"] = g("L0_ffn_w1"); m["mw2"] = g("L0_ffn_w2")
    m["mb1t"] = np.ascontiguousarray(g("L0_ffn_b1").reshape(NE, 32, 128).transpose(0, 2, 1))
    m["mb2t"] = np.ascontiguousarray(g("L0_ffn_b2").reshape(NE, 8, 128).transpose(0, 2, 1))
    wr = np.zeros((128, 8, 128), np.float32)
    wr[:, :, 0:16] = g("L0_ffn_wg_red").reshape(16, 8, 128).transpose(2, 1, 0)
    m["wgredT"] = wr.reshape(128, 1024)
    m["wg"] = g("L0_ffn_wg")
    m.update(make_consts())
    return {k: np.ascontiguousarray(v) for k, v in m.items()}


NSEQ = 8192
_CACHE = {}


def kernel(**inputs):
    if "nc" not in _CACHE:
        _CACHE["nc"], _ = build(NSEQ)
    nc = _CACHE["nc"]
    in_maps = [prep_inputs(inputs, NSEQ, b) for b in range(8)]
    res = run_bass_kernel_spmd(nc, in_maps, core_ids=list(range(8)))
    return np.stack([np.asarray(res.results[b]["out"]) for b in range(8)], axis=0)
